# version 40
# baseline (speedup 1.0000x reference)
import numpy as np
from contextlib import ExitStack

import concourse.bass as bass
import concourse.mybir as mybir
from concourse.bass_utils import run_bass_kernel_spmd

F32 = mybir.dt.float32
BF16 = mybir.dt.bfloat16
I32 = mybir.dt.int32
AF = mybir.ActivationFunctionType
ALU = mybir.AluOpType
AX = mybir.AxisListType

D = 1024
NH = 16
DH = 64
LAT = 256
HID = 3072
SEQ = 4096
TOK = 2048
EC = D // 128
LC = LAT // 128
HC = HID // 128
NT_OWN = TOK // 128
NT_SEQ = SEQ // 128
NKT = SEQ // 128
EPS = 1e-5
MSHIFT = -24.0

ENG_NAMES = {"t": "tensor", "v": "vector", "s": "scalar", "g": "gpsimd", "y": "sync"}


class _Op:
    __slots__ = ("eng", "fn", "eord", "deps", "inc", "icount", "isdma", "qkey",
                 "qval", "nextinc", "noinst")

    def __init__(self):
        self.deps = set()
        self.inc = False
        self.icount = None
        self.isdma = False
        self.qkey = None
        self.qval = None
        self.nextinc = None
        self.noinst = False


class Em:

    def __init__(self, nc, block, es):
        self.nc = nc
        self.block = block
        self.es = es
        self.engsem = {e: es.enter_context(nc.semaphore(f"es_{e}")) for e in "tvsg"}
        self.dmasem = {}
        self.pending = {e: [] for e in "tvsgy"}
        self.ecount = {e: 0 for e in "tvsgy"}
        self.inccount = {e: 0 for e in "tvsg"}
        self.qcount = {}
        self.hist = {}
        self.hw = {e: {} for e in "tvsgy"}

    def _chsem(self, ch):
        if ch not in self.dmasem:
            self.dmasem[ch] = self.es.enter_context(self.nc.semaphore(f"dc_{ch}"))
            self.qcount[ch] = 0
        return self.dmasem[ch]

    @staticmethod
    def r(buf, c0, c1):
        off = buf.get("off", 0)
        lo = (off + c0) * buf["isz"]
        hi = (off + c1) * buf["isz"]
        if buf["psum"]:
            lo = (lo // 2048) * 2048
            hi = ((hi + 2047) // 2048) * 2048
        return (buf["key"], lo, hi)

    def _dep(self, o, src):
        if src is o:
            return
        if src.isdma:
            o.deps.add(src)
            return
        if src.eng == "y":
            raise AssertionError("sync engine op as dep target")
        if src.eng == o.eng and not o.isdma:
            if o.eng == "t":
                return
        o.deps.add(src)
        src.inc = True

    def _access(self, o, regions, write):
        for (key, lo, hi) in regions:
            h = self.hist.setdefault(key, {"w": [], "r": []})
            for (so, slo, shi) in h["w"]:
                if slo < hi and lo < shi:
                    self._dep(o, so)
            if write:
                for (so, slo, shi) in h["r"]:
                    if slo < hi and lo < shi:
                        self._dep(o, so)
                h["w"] = [e for e in h["w"] if not (lo <= e[1] and e[2] <= hi)]
                h["r"] = [e for e in h["r"] if not (lo <= e[1] and e[2] <= hi)]
                h["w"].append((o, lo, hi))
            else:
                dom = o.qkey if o.isdma else o.eng
                h["r"] = [e for e in h["r"]
                          if not ((lo <= e[1] and e[2] <= hi) and
                                  ((e[0].qkey if e[0].isdma else e[0].eng) == dom))]
                h["r"].append((o, lo, hi))

    def op(self, eng, fn, r=(), w=()):
        o = _Op()
        o.eng = eng
        o.fn = fn
        o.eord = self.ecount[eng]
        self.ecount[eng] += 1
        self._access(o, r, False)
        self._access(o, w, True)
        self.pending[eng].append(o)
        return o

    def dma(self, eng, ch, fn, r=(), w=()):
        self._chsem(ch)
        o = _Op()
        o.eng = eng
        o.fn = fn
        o.eord = self.ecount[eng]
        self.ecount[eng] += 1
        o.isdma = True
        o.qkey = ch
        self.qcount[ch] += 16
        o.qval = self.qcount[ch]
        self._access(o, r, False)
        self._access(o, w, True)
        self.pending[eng].append(o)
        return o

    def raw(self, eng, fn):
        o = _Op()
        o.eng = eng
        o.fn = fn
        o.eord = self.ecount[eng]
        self.ecount[eng] += 1
        o.noinst = True
        self.pending[eng].append(o)
        return o

    def _waits(self, o):
        acc = {}
        for d in o.deps:
            if d.isdma:
                sem = self.dmasem[d.qkey]
                val = d.qval
            else:
                tgt = d if d.icount is not None else d.nextinc
                sem = self.engsem[d.eng]
                val = tgt.icount
            k = id(sem)
            if k not in acc or acc[k][1] < val:
                acc[k] = (sem, val)
        return list(acc.values())

    def flush(self):
        for e in "tvsg":
            ops = [o for o in self.pending[e] if not o.noinst and not o.isdma]
            if ops:
                ops[-1].inc = True
            for o in ops:
                if o.inc:
                    self.inccount[e] += 1
                    o.icount = self.inccount[e]
            nxt = None
            for o in reversed(ops):
                if o.icount is not None:
                    nxt = o
                o.nextinc = nxt
        for e in ("y", "g", "v", "s", "t"):
            ops = self.pending[e]
            if not ops:
                continue
            hw = self.hw[e]
            dmsems = self.dmasem
            emsems = self.engsem

            def body(eng, ops=ops, e=e, hw=hw):
                for o in ops:
                    if o.noinst:
                        o.fn(eng)
                        continue
                    for (sem, val) in self._waits(o):
                        k = id(sem)
                        if hw.get(k, -1) < val:
                            eng.wait_ge(sem, val)
                            hw[k] = val
                    inst = o.fn(eng)
                    if o.isdma:
                        inst.then_inc(dmsems[o.qkey], 16)
                    elif o.icount is not None:
                        inst.then_inc(emsems[e], 1)

            getattr(self.block, ENG_NAMES[e])(body)
            self.pending[e] = []

    def barrier(self):
        self.flush()
        counts = dict(self.inccount)
        qs = dict(self.qcount)

        for e in "tvsgy":
            def f(eng, e=e, counts=counts, qs=qs):
                for f2 in "tvsg":
                    if f2 != e and counts[f2] > 0:
                        eng.wait_ge(self.engsem[f2], counts[f2])
                for q, v in qs.items():
                    if v > 0:
                        eng.wait_ge(self.dmasem[q], v)
            self.raw(e, f)
            for f2 in "tvsg":
                if f2 != e:
                    k = id(self.engsem[f2])
                    self.hw[e][k] = max(self.hw[e].get(k, -1), counts[f2])
            for q, v in qs.items():
                k = id(self.dmasem[q])
                self.hw[e][k] = max(self.hw[e].get(k, -1), v)


_ISZ = {F32: 4, BF16: 2, I32: 4}


class Arena:

    def __init__(self, lo, hi):
        self.spans = [(lo, hi)]

    def alloc(self, n, align=32):
        n = (n + align - 1) // align * align
        for i, (lo, hi) in enumerate(self.spans):
            alo = (lo + align - 1) // align * align
            if hi - alo >= n:
                pre = (lo, alo) if alo > lo else None
                post = (alo + n, hi) if alo + n < hi else None
                repl = ([pre] if pre else []) + ([post] if post else [])
                self.spans[i:i + 1] = repl
                return (alo, alo + n)
        raise MemoryError(f"sbuf arena: cannot alloc {n} bytes; free={self.spans}")

    def free(self, rng):
        lo, hi = rng
        self.spans.append((lo, hi))
        self.spans.sort()
        merged = []
        for s in self.spans:
            if merged and merged[-1][1] >= s[0]:
                merged[-1] = (merged[-1][0], max(merged[-1][1], s[1]))
            else:
                merged.append(list(s) and s)
        self.spans = [tuple(s) for s in merged]


class Grp:
    def __init__(self, arena):
        self.arena = arena
        self.rngs = []

    def close(self):
        for r_ in self.rngs:
            self.arena.free(r_)
        self.rngs = []


def build_program():
    nc = bass.Bass("TRN2", target_bir_lowering=False)

    xown = nc.dram_tensor("xown", [TOK, D], F32, kind="ExternalInput").ap()
    xoth = nc.dram_tensor("xoth", [TOK, D], F32, kind="ExternalInput").ap()
    wq = nc.dram_tensor("wq", [D, D], F32, kind="ExternalInput").ap()
    bq = nc.dram_tensor("bq", [D], F32, kind="ExternalInput").ap()
    wl = nc.dram_tensor("wl", [D, LAT], F32, kind="ExternalInput").ap()
    bl = nc.dram_tensor("bl", [LAT], F32, kind="ExternalInput").ap()
    wk = nc.dram_tensor("wk", [LAT, D], F32, kind="ExternalInput").ap()
    bk = nc.dram_tensor("bk", [D], F32, kind="ExternalInput").ap()
    wv = nc.dram_tensor("wv", [LAT, D], F32, kind="ExternalInput").ap()
    bv = nc.dram_tensor("bv", [D], F32, kind="ExternalInput").ap()
    wo = nc.dram_tensor("wo", [D, D], F32, kind="ExternalInput").ap()
    bo = nc.dram_tensor("bo", [D], F32, kind="ExternalInput").ap()
    wg = nc.dram_tensor("wg", [HID, D], I32, kind="ExternalInput").ap()
    wu = nc.dram_tensor("wu", [HID, D], I32, kind="ExternalInput").ap()
    wd = nc.dram_tensor("wd", [D, HID], I32, kind="ExternalInput").ap()
    out = nc.dram_tensor("out", [TOK, D], F32, kind="ExternalOutput").ap()
    xp = nc.dram_tensor("xp", [TOK, D], F32, kind="Internal").ap()
    xpd = {"key": "xpd", "isz": 1, "psum": False}

    top = ExitStack()
    with top:
        block = top.enter_context(nc.Block())
        em = Em(nc, block, top)

        POOLW = 52800
        pool_h = top.enter_context(nc.sbuf_tensor("pool", [128, POOLW], I32))
        arena = Arena(0, POOLW * 4)
        perm = Grp(arena)

        def SB(name, shape, dt=F32, grp=None):
            ncols = int(np.prod(shape[1:]))
            nb = ncols * _ISZ[dt]
            rng = arena.alloc(nb)
            (grp or perm).rngs.append(rng)
            base = pool_h[:, rng[0] // 4: rng[1] // 4].bitcast(dt)
            if shape[0] != 128:
                base = base[0:shape[0], :]
            base = base[:, 0:ncols]
            return {"a": base, "key": name, "isz": _ISZ[dt], "psum": False, "cols": ncols}

        def PS(name, shape, dt=F32, stack=None):
            h = (stack or top).enter_context(nc.psum_tensor(name, shape, dt))
            return {"a": h, "key": name, "isz": _ISZ[dt], "psum": True,
                    "cols": int(np.prod(shape[1:]))}

        def _lp(f):
            def g(eng):
                with nc.allow_low_precision(reason="softmax 1/Z broadcast uses bf16"):
                    return f(eng)
            return g

        def full(b):
            return Em.r(b, 0, b["cols"])

        def view(buf, off, cols):
            return {"a": buf["a"][:, off:off + cols], "key": buf["key"],
                    "isz": buf["isz"], "psum": buf["psum"], "cols": cols,
                    "off": buf.get("off", 0) + off}

        rr = Em.r
        ap = lambda b: b["a"]

        ident = SB("ident", [128, 128], BF16)
        onescol = SB("onescol", [128, 1], BF16)
        onesw = SB("onesw", [65, 128], BF16)
        negm = SB("negm", [128, 1], F32)
        epsb = SB("epsb", [128, 1], F32)

        es_ac = Grp(arena)
        es_bd = Grp(arena)
        bk_sb = SB("bk_sb", [128, EC], F32, es_bd)
        bv16 = SB("bv16", [128, EC], BF16, es_bd)
        bo_row = SB("bo_row", [1, D], F32, es_bd)
        latT = [SB(f"latT{l}", [128, SEQ], BF16, es_bd) for l in range(LC)]
        qT = [SB(f"qT{e}", [128, TOK], BF16, es_bd) for e in range(EC)]
        wk_sb = SB("wk_sb", [128, LC * D], BF16, es_bd)
        wv_sb = SB("wv_sb", [128, LC * D], BF16, es_bd)
        iotaA = SB("iotaA", [128, 128], I32, es_ac)
        iotaB = SB("iotaB", [128, 128], I32, es_ac)
        bq_sb = SB("bq_sb", [128, EC], F32, es_ac)
        bl_sb = SB("bl_sb", [128, LC], F32, es_ac)

        em.op("g", lambda g: g.iota(ap(iotaA)[:], [[1, 128]], channel_multiplier=0),
              w=[full(iotaA)])
        em.op("g", lambda g: g.iota(ap(iotaB)[:], [[0, 128]], channel_multiplier=1),
              w=[full(iotaB)])
        em.op("v", lambda v: v.tensor_tensor(ap(ident)[:], ap(iotaA)[:], ap(iotaB)[:],
                                             ALU.is_equal),
              r=[full(iotaA), full(iotaB)], w=[full(ident)])
        em.op("g", lambda g: g.memset(ap(onescol)[:], 1.0), w=[full(onescol)])
        em.op("g", lambda g: g.memset(ap(onesw)[:], 1.0), w=[full(onesw)])
        em.op("g", lambda g: g.memset(ap(negm)[:], MSHIFT), w=[full(negm)])
        em.op("g", lambda g: g.memset(ap(epsb)[:], EPS), w=[full(epsb)])

        em.dma("g", "bq_sb", lambda g: g.dma_start(ap(bq_sb)[:], bq.rearrange("(c p) -> p c", p=128),
                                                allow_slow_non_contiguous=True), w=[full(bq_sb)])
        em.dma("g", "bk_sb", lambda g: g.dma_start(ap(bk_sb)[:], bk.rearrange("(c p) -> p c", p=128),
                                                allow_slow_non_contiguous=True), w=[full(bk_sb)])
        em.dma("g", "bl_sb", lambda g: g.dma_start(ap(bl_sb)[:], bl.rearrange("(c p) -> p c", p=128),
                                                allow_slow_non_contiguous=True), w=[full(bl_sb)])
        em.dma("g", "bv16", lambda g: g.dma_start(ap(bv16)[:], bv.rearrange("(c p) -> p c", p=128),
                                                allow_slow_non_contiguous=True), w=[full(bv16)])
        em.dma("g", "bo_row", lambda g: g.dma_start(ap(bo_row)[:], bo.rearrange("(a n) -> a n", a=1)),
               w=[full(bo_row)])

        def wload(dst, src, chunks):
            em.dma("g", dst["key"],
                   lambda g, d=dst, s=src, c=chunks: g.dma_start(
                       ap(d)[:].rearrange("p (c n) -> p c n", c=c),
                       s.rearrange("(c p) n -> p c n", p=128)),
                   w=[full(dst)])

        wq_sb = SB("wq_sb", [128, EC * D], BF16, es_ac)
        wl_sb = SB("wl_sb", [128, EC * LAT], BF16, es_ac)
        NXS = 6
        xs = [SB(f"xs{i}", [128, D], F32, es_ac) for i in range(NXS)]
        stt = [SB(f"stt{i}", [128, 12], F32, es_ac) for i in range(NXS)]
        mvb = [SB(f"mvb{i}", [128, 2], F32, es_ac) for i in range(NXS)]
        sdb = [SB(f"sdb{i}", [128, 1], F32, es_ac) for i in range(NXS)]
        rsb = [SB(f"rsb{i}", [128, 1], F32, es_ac) for i in range(NXS)]
        xnt = [SB(f"xnt{i}", [128, D], BF16, es_ac) for i in range(2)]
        xn1T_t = SB("xn1T", [128, EC * SEQ], BF16, es_ac)
        xn1T = [view(xn1T_t, e * SEQ, SEQ) for e in range(EC)]

        ps_ac = ExitStack()
        top.enter_context(ps_ac)
        ps_tr = PS("ps_tr", [128, 2048], BF16, ps_ac)
        ps_mm = PS("ps_mm", [128, 1024], F32, ps_ac)

        wload(wq_sb, wq, EC)
        wload(wl_sb, wl, EC)
        wload(wk_sb, wk, LC)
        wload(wv_sb, wv, LC)

        def ln_s1(tile, st, mv, sd):
            em.op("v", lambda v, a=tile, b=st: v.bn_stats(ap(b)[:, 0:6], ap(a)[:, 0:512]),
                  r=[rr(tile, 0, 512)], w=[rr(st, 0, 6)])
            em.op("v", lambda v, a=tile, b=st: v.bn_stats(ap(b)[:, 6:12], ap(a)[:, 512:1024]),
                  r=[rr(tile, 512, 1024)], w=[rr(st, 6, 12)])
            em.op("v", lambda v, b=st, m=mv: v.bn_aggr(ap(m)[:], ap(b)[:, 0:12]),
                  r=[rr(st, 0, 12)], w=[full(mv)])
            em.op("s", lambda s, m=mv, d=sd: s.activation(ap(d)[:], ap(m)[:, 1:2], AF.Sqrt,
                                                          bias=ap(epsb)[:]),
                  r=[rr(mv, 1, 2), full(epsb)], w=[full(sd)])

        def ln_rec(sd, rs):
            em.op("v", lambda v, d=sd, r_=rs: v.reciprocal(ap(r_)[:], ap(d)[:]),
                  r=[full(sd)], w=[full(rs)])

        def ln_stats(tile, st, mv, sd, rs):
            em.op("v", lambda v, a=tile, b=st: v.bn_stats(ap(b)[:, 0:6], ap(a)[:, 0:512]),
                  r=[rr(tile, 0, 512)], w=[rr(st, 0, 6)])
            em.op("v", lambda v, a=tile, b=st: v.bn_stats(ap(b)[:, 6:12], ap(a)[:, 512:1024]),
                  r=[rr(tile, 512, 1024)], w=[rr(st, 6, 12)])
            em.op("v", lambda v, b=st, m=mv: v.bn_aggr(ap(m)[:], ap(b)[:, 0:12]),
                  r=[rr(st, 0, 12)], w=[full(mv)])
            em.op("s", lambda s, m=mv, d=sd: s.activation(ap(d)[:], ap(m)[:, 1:2], AF.Sqrt,
                                                          bias=ap(epsb)[:]),
                  r=[rr(mv, 1, 2), full(epsb)], w=[full(sd)])
            em.op("v", lambda v, d=sd, r_=rs: v.reciprocal(ap(r_)[:], ap(d)[:]),
                  r=[full(sd)], w=[full(rs)])

        def transpose_tile(src_tile, dst_t, width, col, ping, ps_pool, nch=EC):
            half = 1024 * ping
            for e in range(nch):
                em.op("t", lambda t, a=src_tile, e=e, h=half, p=ps_pool: t.transpose(
                    ap(p)[:, h + e * 128: h + (e + 1) * 128],
                    ap(a)[:, e * 128:(e + 1) * 128], ap(ident)[:]),
                    r=[rr(src_tile, e * 128, (e + 1) * 128), full(ident)],
                    w=[rr(ps_pool, half + e * 128, half + (e + 1) * 128)])
            em.op("v", lambda v, h=half, d=dst_t, c=col, p=ps_pool, n=nch, wd=width:
                  v.tensor_copy(
                      ap(d).rearrange("p (e n) -> p e n", e=ap(d).shape[-1] // wd)[:, 0:n, c:c + 128],
                      ap(p)[:, h:h + n * 128].rearrange("p (e n) -> p e n", e=n)),
                  r=[rr(ps_pool, half, half + nch * 128)],
                  w=[full(dst_t)])

        def a_finish(j):
            buf = j % NXS
            ln_rec(sdb[buf], rsb[buf])
            em.op("v", lambda v, b=buf, o=j % 2: v.tensor_scalar(
                ap(xnt[o])[:], ap(xs[b])[:], ap(mvb[b])[:, 0:1], ap(rsb[b])[:],
                ALU.subtract, ALU.mult),
                r=[full(xs[buf]), rr(mvb[buf], 0, 1), full(rsb[buf])],
                w=[full(xnt[j % 2])])
            transpose_tile(xnt[j % 2], xn1T_t, SEQ, j * 128, j % 2, ps_tr)

        for st_i in range(NT_SEQ):
            buf = st_i % NXS
            src = xown if st_i < NT_OWN else xoth
            row = (st_i % NT_OWN) * 128
            em.dma("y", f"xs{buf}", lambda s, b=buf, sr=src, r_=row: s.dma_start(
                ap(xs[b])[:], sr[r_:r_ + 128, :]), w=[full(xs[buf])])
            ln_s1(xs[buf], stt[buf], mvb[buf], sdb[buf])
            if st_i >= 2:
                a_finish(st_i - 2)
        a_finish(NT_SEQ - 2)
        a_finish(NT_SEQ - 1)

        pp = [0]

        def proj_fm(dst_list, w_sb, wwidth, kchunks, src_list, src_cols, out_chunks,
                    bias, ncols, col_off=0):
            for oc in range(out_chunks):
                for nb in range(ncols // 512):
                    psl = (pp[0] % 2) * 512
                    pp[0] += 1
                    for kc in range(kchunks):
                        em.op("t", lambda t, oc=oc, nb=nb, kc=kc, psn=psl, w_=w_sb, ww=wwidth,
                              sl=src_list, co=col_off, K=kchunks: t.matmul(
                                  ap(ps_mm)[:, psn:psn + 512],
                                  ap(w_)[:, kc * ww + oc * 128: kc * ww + oc * 128 + 128],
                                  ap(sl[kc])[:, co + nb * 512: co + (nb + 1) * 512],
                                  start=(kc == 0), stop=(kc == K - 1)),
                              r=[rr(w_sb, kc * wwidth + oc * 128, kc * wwidth + oc * 128 + 128),
                                 rr(src_list[kc], col_off + nb * 512, col_off + (nb + 1) * 512)],
                              w=[rr(ps_mm, psl, psl + 512)])
                    em.op("v", lambda v, oc=oc, nb=nb, psn=psl, dl=dst_list, b=bias:
                          v.tensor_scalar(ap(dl[oc])[:, nb * 512:(nb + 1) * 512],
                                          ap(ps_mm)[:, psn:psn + 512],
                                          ap(b)[:, oc:oc + 1], None, ALU.add),
                          r=[rr(ps_mm, psl, psl + 512), rr(bias, oc, oc + 1)],
                          w=[rr(dst_list[oc], nb * 512, (nb + 1) * 512)])

        proj_fm(latT, wl_sb, LAT, EC, xn1T, SEQ, LC, bl_sb, SEQ)

        proj_fm(qT, wq_sb, D, EC, xn1T, SEQ, EC, bq_sb, TOK, col_off=0)

        em.barrier()
        em.flush()
        ps_ac.close()
        es_ac.close()

        es_de = Grp(arena)
        wo_sb = SB("wo_sb", [128, EC * D], BF16, es_de)
        attnout = [SB(f"attnout{e}", [128, TOK], BF16, es_de) for e in range(EC)]
        bo2row = SB("bo2row", [1, D], BF16, es_de)
        bo2bc = SB("bo2bc", [128, D], BF16, es_de)
        es_d = Grp(arena)
        kP2 = [[SB(f"kP{j}_{i}", [128, SEQ], BF16, es_d) for i in range(2)] for j in range(2)]
        vP2 = [SB(f"vP{j}", [128, NKT * 192], BF16, es_d) for j in range(2)]
        pbr = SB("pbr", [128, 6 * 1024], BF16, es_d)
        rzb = SB("rzb", [128, 1024], F32, es_d)

        ps_d = ExitStack()
        top.enter_context(ps_d)
        ps_S = PS("ps_S", [128, 2048], F32, ps_d)
        ps_PV = PS("ps_PV", [128, 1024], F32, ps_d)
        ps_B = PS("ps_B", [128, 1024], F32, ps_d)

        wload(wo_sb, wo, EC)

        for half_i in range(2):
            for kc in range(EC):
                em.op("t", lambda t, kc=kc, hi=half_i: t.matmul(
                    ap(ps_B)[0:1, hi * 512:(hi + 1) * 512], ap(bv16)[:, kc:kc + 1],
                    ap(wo_sb)[:, kc * D + hi * 512: kc * D + (hi + 1) * 512],
                    start=(kc == 0), stop=(kc == EC - 1)),
                    r=[rr(bv16, kc, kc + 1), rr(wo_sb, kc * D + half_i * 512,
                                                kc * D + (half_i + 1) * 512)],
                    w=[rr(ps_B, half_i * 512, (half_i + 1) * 512)])
        em.op("v", lambda v: v.tensor_tensor(ap(bo2row)[:], ap(ps_B)[0:1, 0:1024],
                                             ap(bo_row)[:], ALU.add),
              r=[rr(ps_B, 0, 1024), full(bo_row)], w=[full(bo2row)])
        for half_i in range(2):
            em.op("t", lambda t, hi=half_i: t.matmul(
                ap(ps_B)[:, hi * 512:(hi + 1) * 512], ap(onesw)[0:1, 0:128],
                ap(bo2row)[0:1, hi * 512:(hi + 1) * 512], start=True, stop=True),
                r=[full(onesw), rr(bo2row, half_i * 512, (half_i + 1) * 512)],
                w=[rr(ps_B, half_i * 512, (half_i + 1) * 512)])
        em.op("v", lambda v: v.tensor_copy(ap(bo2bc)[:], ap(ps_B)[:, 0:1024]),
              r=[rr(ps_B, 0, 1024)], w=[full(bo2bc)])

        for j in range(2):
            em.op("g", lambda g, j=j: g.memset(ap(kP2[j][0])[64:128, :], 0.0),
                  w=[full(kP2[j][0])])
            em.op("g", lambda g, j=j: g.memset(ap(kP2[j][1])[0:64, :], 0.0),
                  w=[full(kP2[j][1])])
            em.op("g", lambda g, j=j: g.memset(
                ap(vP2[j]).rearrange("p (k n) -> p k n", k=NKT)[:, :, 64:128], 1.0),
                w=[full(vP2[j])])

        pvs = SB("pvs", [128, 1024], F32, es_d)
        def kv_pre(hp, kP, vP):
            for nb in range(SEQ // 512):
                psl = (nb % 2) * 512
                ops = []
                for kc in range(LC):
                    yield lambda kc=kc, nb=nb, psl=psl, hp=hp: em.op(
                        "t", lambda t, nb=nb, kc=kc, psn=psl, hp=hp: t.matmul(
                            ap(ps_B)[:, psn:psn + 512],
                            ap(wk_sb)[:, kc * D + hp * 128: kc * D + hp * 128 + 128],
                            ap(latT[kc])[:, nb * 512:(nb + 1) * 512],
                            start=(kc == 0), stop=(kc == LC - 1)),
                        r=[rr(wk_sb, kc * D + hp * 128, kc * D + hp * 128 + 128),
                           rr(latT[kc], nb * 512, (nb + 1) * 512)],
                        w=[rr(ps_B, psl, psl + 512)])
                for i in range(2):
                    yield lambda i=i, nb=nb, psl=psl, hp=hp, kP=kP: em.op(
                        "v", lambda v, nb=nb, psn=psl, hp=hp, i=i, kP=kP: v.tensor_scalar(
                            ap(kP[i])[64 * i:64 * i + 64, nb * 512:(nb + 1) * 512],
                            ap(ps_B)[64 * i:64 * i + 64, psn:psn + 512],
                            ap(bk_sb)[64 * i:64 * i + 64, hp:hp + 1], None, ALU.add),
                        r=[rr(ps_B, psl, psl + 512), rr(bk_sb, hp, hp + 1)],
                        w=[rr(kP[i], nb * 512, (nb + 1) * 512)])
            for kt in range(NKT):
                psl = (kt % 2) * 512
                for kc in range(LC):
                    yield lambda kt=kt, kc=kc, psl=psl, hp=hp: em.op(
                        "t", lambda t, kt=kt, kc=kc, psn=psl, hp=hp: t.matmul(
                            ap(ps_B)[:, psn:psn + 128],
                            ap(latT[kc])[:, kt * 128:(kt + 1) * 128],
                            ap(wv_sb)[:, kc * D + hp * 128: kc * D + hp * 128 + 128],
                            start=(kc == 0), stop=(kc == LC - 1)),
                        r=[rr(latT[kc], kt * 128, (kt + 1) * 128),
                           rr(wv_sb, kc * D + hp * 128, kc * D + hp * 128 + 128)],
                        w=[rr(ps_B, psl, psl + 128)])
                for i in range(2):
                    yield lambda kt=kt, psl=psl, i=i, vP=vP: em.op(
                        "v", lambda v, kt=kt, psn=psl, i=i, vP=vP: v.tensor_copy(
                            ap(vP)[:, kt * 192 + 128 * i:kt * 192 + 128 * i + 64],
                            ap(ps_B)[:, psn + 64 * i:psn + 64 * i + 64]),
                        r=[rr(ps_B, psl, psl + 128)],
                        w=[rr(vP, kt * 192 + 128 * i, kt * 192 + 128 * i + 64)])

        for g_op in kv_pre(0, kP2[0], vP2[0]):
            g_op()

        steps = [(hp, side, qh, kt)
                 for hp in range(NH // 2) for side in range(2)
                 for qh in range(2) for kt in range(NKT)]
        NS = len(steps)
        pres = {hp: kv_pre(hp + 1, kP2[(hp + 1) % 2], vP2[(hp + 1) % 2])
                for hp in range(NH // 2 - 1)}

        def s_emit(s):
            hp, side, qh, kt = steps[s]
            kP = kP2[hp % 2]
            so = (s % 2) * 1024
            for qc in range(2):
                em.op("t", lambda t, kt=kt, qc=qc, side=side, so=so, qh=qh, hp=hp, kP=kP: t.matmul(
                    ap(ps_S)[:, so + qc * 512: so + (qc + 1) * 512],
                    ap(kP[side])[:, kt * 128:(kt + 1) * 128],
                    ap(qT[hp])[:, qh * 1024 + qc * 512: qh * 1024 + (qc + 1) * 512],
                    start=True, stop=True),
                    r=[rr(kP[side], kt * 128, (kt + 1) * 128),
                       rr(qT[hp], qh * 1024 + qc * 512, qh * 1024 + (qc + 1) * 512)],
                    w=[rr(ps_S, so + qc * 512, so + (qc + 1) * 512)])

        def exp_emit(s):
            so = (s % 2) * 1024
            slot = (s % 6) * 1024
            em.op("s", lambda sc, so=so, sl=slot: sc.activation(
                ap(pbr)[:, sl:sl + 1024], ap(ps_S)[:, so:so + 1024],
                AF.Exp, bias=ap(negm)[:], scale=0.125),
                r=[rr(ps_S, so, so + 1024), full(negm)],
                w=[rr(pbr, slot, slot + 1024)])

        s_emit(0)
        s_emit(1)
        exp_emit(0)
        deferred = []
        for s in range(NS):
            hp, side, qh, kt = steps[s]
            vP = vP2[hp % 2]
            if s + 2 < NS:
                s_emit(s + 2)
            if s + 1 < NS:
                exp_emit(s + 1)
            if kt % 2 == 0 and deferred:
                for th in deferred.pop(0):
                    th()
            pre = pres.get(hp)
            if pre is not None:
                for _ in range(2):
                    g_op = next(pre, None)
                    if g_op is not None:
                        g_op()
            slot = (s % 6) * 1024
            vco = 192 * kt + 64 * side
            for qc in range(2):
                em.op("t", lambda t, kt=kt, qc=qc, vP=vP, vco=vco, sl=slot: t.matmul(
                    ap(ps_PV)[:, qc * 512:(qc + 1) * 512],
                    ap(vP)[:, vco:vco + 128],
                    ap(pbr)[:, sl + qc * 512: sl + (qc + 1) * 512],
                    start=(kt == 0), stop=(kt == NKT - 1)),
                    r=[rr(vP, vco, vco + 128),
                       rr(pbr, slot + qc * 512, slot + (qc + 1) * 512)],
                    w=[rr(ps_PV, qc * 512, (qc + 1) * 512)])
            if kt == NKT - 1:
                em.op("v", lambda v: v.tensor_copy(ap(pvs)[0:128, :],
                                                   ap(ps_PV)[0:128, 0:1024]),
                      r=[rr(ps_PV, 0, 1024)], w=[full(pvs)])

                zr = 64 * (1 - side)
                vr = 64 * side

                def th_rz(c, zr=zr, vr=vr):
                    em.op("v", lambda v, c=c: v.reciprocal(
                        ap(rzb)[vr:vr + 64, c * 256:(c + 1) * 256],
                        ap(pvs)[zr:zr + 64, c * 256:(c + 1) * 256]),
                        r=[full(pvs)], w=[full(rzb)])

                def th_tt(hp=hp, qh=qh, vr=vr):
                    em.op("v", lambda v: v.tensor_tensor(
                        ap(attnout[hp])[vr:vr + 64, qh * 1024:(qh + 1) * 1024],
                        ap(pvs)[vr:vr + 64, 0:1024],
                        ap(rzb)[vr:vr + 64, 0:1024], ALU.mult),
                        r=[full(pvs), full(rzb)],
                        w=[full(attnout[hp])])

                deferred = [[lambda c=c: th_rz(c)] for c in range(4)] + [[th_tt]]

        while deferred:
            for th in deferred.pop(0):
                th()
        em.barrier()
        em.flush()
        ps_d.close()
        es_d.close()
        es_bd.close()

        es_wds = Grp(arena)
        wd_stage = [SB(f"wds{i}", [128, HID], BF16, es_wds) for i in range(2)]
        es_wdT = Grp(arena)
        wdT_t = SB("wdT", [128, HC * D], BF16, es_wdT)
        wdT = [view(wdT_t, h * D, D) for h in range(HC)]
        for et in range(2):
            em.dma("g", f"wds{et}", lambda g, et=et: g.dma_start(
                ap(wd_stage[et])[:], wd[et * 128:(et + 1) * 128, :]),
                w=[full(wd_stage[et])])
        es_xn2 = Grp(arena)
        xn2T_t = SB("xn2T", [128, EC * TOK], BF16, es_xn2)
        xn2T = [view(xn2T_t, e * TOK, TOK) for e in range(EC)]
        es_e = Grp(arena)
        y1s = [SB(f"y1s{i}", [128, D], F32, es_e) for i in range(3)]
        xt2 = [SB(f"xt2{i}", [128, D], F32, es_e) for i in range(4)]
        xps = [SB(f"xps{i}", [128, D], F32, es_e) for i in range(4)]
        xpn = [SB(f"xpn{i}", [128, D], F32, es_e) for i in range(1)]
        xn2t = [SB(f"xn2t{i}", [128, D], BF16, es_e) for i in range(3)]
        st2 = [SB(f"st2_{i}", [128, 12], F32, es_e) for i in range(3)]
        mv2 = [SB(f"mv2_{i}", [128, 2], F32, es_e) for i in range(3)]
        sd2 = [SB(f"sd2_{i}", [128, 1], F32, es_e) for i in range(3)]
        rs2 = [SB(f"rs2_{i}", [128, 1], F32, es_e) for i in range(3)]
        st3 = [SB(f"st3_{i}", [128, 12], F32, es_e) for i in range(3)]
        mv3 = [SB(f"mv3_{i}", [128, 2], F32, es_e) for i in range(3)]
        sd3 = [SB(f"sd3_{i}", [128, 1], F32, es_e) for i in range(3)]
        rs3 = [SB(f"rs3_{i}", [128, 1], F32, es_e) for i in range(3)]

        ps_e = ExitStack()
        top.enter_context(ps_e)
        ps_y1 = PS("ps_y1", [128, 3072], F32, ps_e)
        ps_tr2 = PS("ps_tr2", [128, 2048], BF16, ps_e)

        def wd_transpose(et, buf):
            for g in range(HC // 8):
                for j in range(8):
                    hc = g * 8 + j
                    em.op("t", lambda t, b=buf, hc=hc, j=j: t.transpose(
                        ap(ps_tr2)[:, 1024 + j * 128: 1024 + j * 128 + 128],
                        ap(wd_stage[b])[:, hc * 128:(hc + 1) * 128], ap(ident)[:]),
                        r=[rr(wd_stage[buf], hc * 128, (hc + 1) * 128), full(ident)],
                        w=[rr(ps_tr2, 1024 + j * 128, 1024 + j * 128 + 128)])
                em.op("v", lambda v, g=g, et=et: v.tensor_copy(
                    ap(wdT_t).rearrange("p (h n) -> p h n", h=HC)[:, g * 8:(g + 1) * 8,
                                                                 et * 128:(et + 1) * 128],
                    ap(ps_tr2)[:, 1024:2048].rearrange("p (j n) -> p j n", j=8)),
                    r=[rr(ps_tr2, 1024, 2048)],
                    w=[full(wdT_t)])

        for tt in range(NT_OWN):
            buf = tt % 3
            bx = tt % 4
            for half_i in range(2):
                for dc in range(EC):
                    em.op("t", lambda t, tt=tt, dc=dc, hi=half_i, b=buf: t.matmul(
                        ap(ps_y1)[:, b * 1024 + hi * 512: b * 1024 + (hi + 1) * 512],
                        ap(attnout[dc])[:, tt * 128:(tt + 1) * 128],
                        ap(wo_sb)[:, dc * D + hi * 512: dc * D + (hi + 1) * 512],
                        start=(dc == 0), stop=(dc == EC - 1)),
                        r=[rr(attnout[dc], tt * 128, (tt + 1) * 128),
                           rr(wo_sb, dc * D + half_i * 512, dc * D + (half_i + 1) * 512)],
                        w=[rr(ps_y1, buf * 1024 + half_i * 512, buf * 1024 + (half_i + 1) * 512)])
            if tt >= 1:
                pv_ = (tt - 1) % 3
                transpose_tile(xn2t[pv_], xn2T_t, TOK, (tt - 1) * 128, 0, ps_tr2)
            if tt % 2 == 0 and tt // 2 < EC:
                et = tt // 2
                wd_transpose(et, et % 2)
                if et + 2 < EC:
                    em.dma("g", f"wds{et % 2}", lambda g, et=et: g.dma_start(
                        ap(wd_stage[et % 2])[:], wd[(et + 2) * 128:(et + 3) * 128, :]),
                        w=[full(wd_stage[et % 2])])
            em.op("v", lambda v, b=buf: v.tensor_tensor(
                ap(y1s[b])[:], ap(ps_y1)[:, b * 1024:(b + 1) * 1024], ap(bo2bc)[:], ALU.add),
                r=[rr(ps_y1, buf * 1024, (buf + 1) * 1024), full(bo2bc)],
                w=[full(y1s[buf])])
            ln_stats(y1s[buf], st2[buf], mv2[buf], sd2[buf], rs2[buf])
            em.dma("y", f"xt2{bx}", lambda s, tt=tt, b=bx: s.dma_start(
                ap(xt2[b])[:], xown[tt * 128:(tt + 1) * 128, :]), w=[full(xt2[bx])])
            em.op("v", lambda v, b=buf: v.tensor_scalar(
                ap(xpn[0])[:], ap(y1s[b])[:], ap(mv2[b])[:, 0:1], ap(rs2[b])[:],
                ALU.subtract, ALU.mult),
                r=[full(y1s[buf]), rr(mv2[buf], 0, 1), full(rs2[buf])], w=[full(xpn[0])])
            em.op("v", lambda v, x4=bx: v.tensor_tensor(
                ap(xps[x4])[:], ap(xpn[0])[:], ap(xt2[x4])[:], ALU.add),
                r=[full(xpn[0]), full(xt2[bx])], w=[full(xps[bx])])
            em.dma("y", f"xps{bx}", lambda s, tt=tt, b=bx: s.dma_start(
                xp[tt * 128:(tt + 1) * 128, :], ap(xps[b])[:]),
                r=[full(xps[bx])], w=[rr(xpd, tt, tt + 1)])
            ln_stats(xps[bx], st3[buf], mv3[buf], sd3[buf], rs3[buf])
            em.op("v", lambda v, b=buf, x4=bx: v.tensor_scalar(
                ap(xn2t[b])[:], ap(xps[x4])[:], ap(mv3[b])[:, 0:1], ap(rs3[b])[:],
                ALU.subtract, ALU.mult),
                r=[full(xps[bx]), rr(mv3[buf], 0, 1), full(rs3[buf])], w=[full(xn2t[buf])])
        transpose_tile(xn2t[(NT_OWN - 1) % 3], xn2T_t, TOK, (NT_OWN - 1) * 128, 0, ps_tr2)

        em.barrier()
        em.flush()
        ps_e.close()
        es_e.close()
        es_de.close()
        es_wds.close()

        es_f = Grp(arena)
        hT = [SB(f"hT{h}", [128, TOK], BF16, es_f) for h in range(HC)]
        wg_st = [SB(f"wgst{i}", [128, D], BF16, es_f) for i in range(2)]
        wu_st = [SB(f"wust{i}", [128, D], BF16, es_f) for i in range(2)]
        wgT = [SB(f"wgT{e}", [128, 256], BF16, es_f) for e in range(EC)]
        wuT = [SB(f"wuT{e}", [128, 256], BF16, es_f) for e in range(EC)]
        sg = [SB(f"sg{i}", [128, 2048], BF16, es_f) for i in range(2)]

        ps_f = ExitStack()
        top.enter_context(ps_f)
        ps_g = PS("ps_g", [128, 2048], F32, ps_f)
        ps_u = PS("ps_u", [128, 1024], F32, ps_f)
        ps_ta = PS("ps_ta", [128, 1024], BF16, ps_f)
        ps_tb = PS("ps_tb", [128, 1024], BF16, ps_f)

        def stage_w(dst, src, ht):
            em.dma("g", dst["key"], lambda g, d=dst, s=src, ht=ht: g.dma_start(
                ap(d)[:], s[ht * 128:(ht + 1) * 128, :]), w=[full(dst)])

        def wtr(stage, win, ht, pst):
            slot = (ht % 2) * 128
            for e in range(EC):
                em.op("t", lambda t, a=stage, e=e, p=pst: t.transpose(
                    ap(p)[:, e * 128:(e + 1) * 128],
                    ap(a)[:, e * 128:(e + 1) * 128], ap(ident)[:]),
                    r=[rr(stage, e * 128, (e + 1) * 128), full(ident)],
                    w=[rr(pst, e * 128, (e + 1) * 128)])
            for e in range(EC):
                em.op("v", lambda v, e=e, d=win[e], sl=slot, p=pst: v.tensor_copy(
                    ap(d)[:, sl:sl + 128], ap(p)[:, e * 128:(e + 1) * 128]),
                    r=[rr(pst, e * 128, (e + 1) * 128)],
                    w=[rr(win[e], slot, slot + 128)])

        stage_w(wg_st[0], wg, 0)
        stage_w(wu_st[0], wu, 0)
        wtr(wg_st[0], wgT, 0, ps_ta)
        wtr(wu_st[0], wuT, 0, ps_tb)

        for ht in range(HC):
            slot = (ht % 2) * 128
            if ht + 1 < HC:
                stage_w(wg_st[(ht + 1) % 2], wg, ht + 1)
                stage_w(wu_st[(ht + 1) % 2], wu, ht + 1)
            for qc in range(4):
                for e in range(EC):
                    em.op("t", lambda t, qc=qc, e=e, sl=slot: t.matmul(
                        ap(ps_g)[:, qc * 512:(qc + 1) * 512],
                        ap(wgT[e])[:, sl:sl + 128],
                        ap(xn2T[e])[:, qc * 512:(qc + 1) * 512],
                        start=(e == 0), stop=(e == EC - 1)),
                        r=[rr(wgT[e], slot, slot + 128), rr(xn2T[e], qc * 512, (qc + 1) * 512)],
                        w=[rr(ps_g, qc * 512, (qc + 1) * 512)])
            if ht + 1 < HC:
                wtr(wg_st[(ht + 1) % 2], wgT, ht + 1, ps_ta)
            em.op("s", lambda s, b=ht % 2: s.activation(
                ap(sg[b])[:], ap(ps_g)[:], AF.Silu, scale=1.0 / 127.0),
                r=[rr(ps_g, 0, 2048)], w=[full(sg[ht % 2])])
            for qc in range(4):
                ups = (qc % 2) * 512
                for e in range(EC):
                    em.op("t", lambda t, qc=qc, e=e, sl=slot, u=ups: t.matmul(
                        ap(ps_u)[:, u:u + 512],
                        ap(wuT[e])[:, sl:sl + 128],
                        ap(xn2T[e])[:, qc * 512:(qc + 1) * 512],
                        start=(e == 0), stop=(e == EC - 1)),
                        r=[rr(wuT[e], slot, slot + 128), rr(xn2T[e], qc * 512, (qc + 1) * 512)],
                        w=[rr(ps_u, ups, ups + 512)])
                em.op("v", lambda v, ht=ht, qc=qc, u=ups, b=ht % 2: v.tensor_tensor(
                    ap(hT[ht])[:, qc * 512:(qc + 1) * 512], ap(ps_u)[:, u:u + 512],
                    ap(sg[b])[:, qc * 512:(qc + 1) * 512], ALU.mult),
                    r=[rr(ps_u, ups, ups + 512), rr(sg[ht % 2], qc * 512, (qc + 1) * 512)],
                    w=[rr(hT[ht], qc * 512, (qc + 1) * 512)])
            if ht + 1 < HC:
                wtr(wu_st[(ht + 1) % 2], wuT, ht + 1, ps_tb)

        em.barrier()
        em.flush()
        ps_f.close()
        es_xn2.close()

        xpr = [SB(f"xpr{i}", [128, D], F32, es_f) for i in range(3)]
        outb = [SB(f"outb{i}", [128, D], F32, es_f) for i in range(3)]
        t4b = [SB(f"t4b{i}", [128, D], F32, es_f) for i in range(2)]
        st4 = [SB(f"st4_{i}", [128, 12], F32, es_f) for i in range(2)]
        mv4 = [SB(f"mv4_{i}", [128, 2], F32, es_f) for i in range(2)]
        sd4 = [SB(f"sd4_{i}", [128, 1], F32, es_f) for i in range(2)]
        rs4 = [SB(f"rs4_{i}", [128, 1], F32, es_f) for i in range(2)]

        ps_f2 = ExitStack()
        top.enter_context(ps_f2)
        ps_y2 = PS("ps_y2", [128, 2048], F32, ps_f2)

        for tt in range(NT_OWN):
            buf = tt % 3
            pb_ = tt % 2
            em.dma("y", f"xpr{buf}", lambda s, tt=tt, b=buf: s.dma_start(
                ap(xpr[b])[:], xp[tt * 128:(tt + 1) * 128, :]),
                r=[rr(xpd, tt, tt + 1)], w=[full(xpr[buf])])
            for half_i in range(2):
                for hc in range(HC):
                    em.op("t", lambda t, tt=tt, hc=hc, hi=half_i, b=pb_: t.matmul(
                        ap(ps_y2)[:, b * 1024 + hi * 512: b * 1024 + (hi + 1) * 512],
                        ap(hT[hc])[:, tt * 128:(tt + 1) * 128],
                        ap(wdT[hc])[:, hi * 512:(hi + 1) * 512],
                        start=(hc == 0), stop=(hc == HC - 1)),
                        r=[rr(hT[hc], tt * 128, (tt + 1) * 128),
                           rr(wdT[hc], half_i * 512, (half_i + 1) * 512)],
                        w=[rr(ps_y2, pb_ * 1024 + half_i * 512, pb_ * 1024 + (half_i + 1) * 512)])
            em.op("v", lambda v, b=pb_, p=pb_: v.bn_stats(ap(st4[b])[:, 0:6],
                                                   ap(ps_y2)[:, p * 1024:p * 1024 + 512]),
                  r=[rr(ps_y2, pb_ * 1024, pb_ * 1024 + 512)], w=[rr(st4[pb_], 0, 6)])
            em.op("v", lambda v, b=pb_, p=pb_: v.bn_stats(ap(st4[b])[:, 6:12],
                                                   ap(ps_y2)[:, p * 1024 + 512:(p + 1) * 1024]),
                  r=[rr(ps_y2, pb_ * 1024 + 512, (pb_ + 1) * 1024)], w=[rr(st4[pb_], 6, 12)])
            em.op("v", lambda v, b=pb_: v.bn_aggr(ap(mv4[b])[:], ap(st4[b])[:, 0:12]),
                  r=[rr(st4[pb_], 0, 12)], w=[full(mv4[pb_])])
            em.op("s", lambda s, b=pb_: s.activation(ap(sd4[b])[:], ap(mv4[b])[:, 1:2], AF.Sqrt,
                                                     bias=ap(epsb)[:]),
                  r=[rr(mv4[pb_], 1, 2), full(epsb)], w=[full(sd4[pb_])])
            em.op("v", lambda v, b=pb_: v.reciprocal(ap(rs4[b])[:], ap(sd4[b])[:]),
                  r=[full(sd4[pb_])], w=[full(rs4[pb_])])
            em.op("v", lambda v, b=pb_, p=pb_: v.tensor_scalar(
                ap(t4b[b])[:], ap(ps_y2)[:, p * 1024:(p + 1) * 1024],
                ap(mv4[b])[:, 0:1], ap(rs4[b])[:], ALU.subtract, ALU.mult),
                r=[rr(ps_y2, pb_ * 1024, (pb_ + 1) * 1024), rr(mv4[pb_], 0, 1), full(rs4[pb_])],
                w=[full(t4b[pb_])])
            em.op("v", lambda v, b=buf, t4=pb_: v.tensor_tensor(
                ap(outb[b])[:], ap(t4b[t4])[:], ap(xpr[b])[:], ALU.add),
                r=[full(t4b[pb_]), full(xpr[buf])], w=[full(outb[buf])])
            em.dma("y", f"outb{buf}", lambda s, tt=tt, b=buf: s.dma_start(
                out[tt * 128:(tt + 1) * 128, :], ap(outb[b])[:]), r=[full(outb[buf])])

        finals = {ch: em.qcount[ch] for ch in em.qcount}
        def _final(eng, finals=finals):
            for ch, v in finals.items():
                eng.wait_ge(em.dmasem[ch], v)
        em.raw("y", _final)
        em.flush()
        ps_f2.close()

    nc.finalize()
    return nc


_NC_CACHE = None


def kernel(L0_attn_x, L0_attn_Wq, L0_attn_bq, L0_attn_Wl, L0_attn_bl,
           L0_attn_Wk, L0_attn_bk, L0_attn_Wv, L0_attn_bv, L0_attn_Wo,
           L0_attn_bo, L0_ffn_w_gate, L0_ffn_w_up, L0_ffn_w_down):
    global _NC_CACHE
    x = np.ascontiguousarray(np.asarray(L0_attn_x, dtype=np.float32))
    c = np.ascontiguousarray
    weights = dict(
        wq=c(np.asarray(L0_attn_Wq, np.float32)), bq=c(np.asarray(L0_attn_bq, np.float32)),
        wl=c(np.asarray(L0_attn_Wl, np.float32)), bl=c(np.asarray(L0_attn_bl, np.float32)),
        wk=c(np.asarray(L0_attn_Wk, np.float32)), bk=c(np.asarray(L0_attn_bk, np.float32)),
        wv=c(np.asarray(L0_attn_Wv, np.float32)), bv=c(np.asarray(L0_attn_bv, np.float32)),
        wo=c(np.asarray(L0_attn_Wo, np.float32)), bo=c(np.asarray(L0_attn_bo, np.float32)),
        wg=c(np.asarray(L0_ffn_w_gate, np.int32)), wu=c(np.asarray(L0_ffn_w_up, np.int32)),
        wd=c(np.asarray(L0_ffn_w_down, np.int32)),
    )
    in_maps = []
    for core in range(8):
        b, half = core // 2, core % 2
        m = dict(weights)
        m["xown"] = c(x[b, half * TOK:(half + 1) * TOK, :])
        m["xoth"] = c(x[b, (1 - half) * TOK:(2 - half) * TOK, :])
        in_maps.append(m)
    if _NC_CACHE is None:
        _NC_CACHE = build_program()
    res = run_bass_kernel_spmd(_NC_CACHE, in_maps, core_ids=list(range(8)))
    outp = np.empty((4, SEQ, D), np.float32)
    for core in range(8):
        b, half = core // 2, core % 2
        outp[b, half * TOK:(half + 1) * TOK, :] = res.results[core]["out"]
    return outp


# version 41
# speedup vs baseline: 1.0948x; 1.0948x over previous
import numpy as np
from contextlib import ExitStack

import concourse.bass as bass
import concourse.mybir as mybir
from concourse.bass_utils import run_bass_kernel_spmd

F32 = mybir.dt.float32
BF16 = mybir.dt.bfloat16
I32 = mybir.dt.int32
AF = mybir.ActivationFunctionType
ALU = mybir.AluOpType
AX = mybir.AxisListType

D = 1024
NH = 16
DH = 64
LAT = 256
HID = 3072
SEQ = 4096
TOK = 2048
EC = D // 128
LC = LAT // 128
HC = HID // 128
NT_OWN = TOK // 128
NT_SEQ = SEQ // 128
NKT = SEQ // 128
EPS = 1e-5
MSHIFT = -24.0

ENG_NAMES = {"t": "tensor", "v": "vector", "s": "scalar", "g": "gpsimd", "y": "sync"}


class _Op:
    __slots__ = ("eng", "fn", "eord", "deps", "inc", "icount", "isdma", "qkey",
                 "qval", "nextinc", "noinst")

    def __init__(self):
        self.deps = set()
        self.inc = False
        self.icount = None
        self.isdma = False
        self.qkey = None
        self.qval = None
        self.nextinc = None
        self.noinst = False


class Em:

    def __init__(self, nc, block, es):
        self.nc = nc
        self.block = block
        self.es = es
        self.engsem = {e: es.enter_context(nc.semaphore(f"es_{e}")) for e in "tvsg"}
        self.dmasem = {}
        self.pending = {e: [] for e in "tvsgy"}
        self.ecount = {e: 0 for e in "tvsgy"}
        self.inccount = {e: 0 for e in "tvsg"}
        self.qcount = {}
        self.hist = {}
        self.hw = {e: {} for e in "tvsgy"}

    def _chsem(self, ch):
        if ch not in self.dmasem:
            self.dmasem[ch] = self.es.enter_context(self.nc.semaphore(f"dc_{ch}"))
            self.qcount[ch] = 0
        return self.dmasem[ch]

    @staticmethod
    def r(buf, c0, c1):
        off = buf.get("off", 0)
        lo = (off + c0) * buf["isz"]
        hi = (off + c1) * buf["isz"]
        if buf["psum"]:
            lo = (lo // 2048) * 2048
            hi = ((hi + 2047) // 2048) * 2048
        return (buf["key"], lo, hi)

    def _dep(self, o, src):
        if src is o:
            return
        if src.isdma:
            o.deps.add(src)
            return
        if src.eng == "y":
            raise AssertionError("sync engine op as dep target")
        if src.eng == o.eng and not o.isdma:
            if o.eng == "t":
                return
        o.deps.add(src)
        src.inc = True

    def _access(self, o, regions, write):
        for (key, lo, hi) in regions:
            h = self.hist.setdefault(key, {"w": [], "r": []})
            for (so, slo, shi) in h["w"]:
                if slo < hi and lo < shi:
                    self._dep(o, so)
            if write:
                for (so, slo, shi) in h["r"]:
                    if slo < hi and lo < shi:
                        self._dep(o, so)
                h["w"] = [e for e in h["w"] if not (lo <= e[1] and e[2] <= hi)]
                h["r"] = [e for e in h["r"] if not (lo <= e[1] and e[2] <= hi)]
                h["w"].append((o, lo, hi))
            else:
                dom = o.qkey if o.isdma else o.eng
                h["r"] = [e for e in h["r"]
                          if not ((lo <= e[1] and e[2] <= hi) and
                                  ((e[0].qkey if e[0].isdma else e[0].eng) == dom))]
                h["r"].append((o, lo, hi))

    def op(self, eng, fn, r=(), w=()):
        o = _Op()
        o.eng = eng
        o.fn = fn
        o.eord = self.ecount[eng]
        self.ecount[eng] += 1
        self._access(o, r, False)
        self._access(o, w, True)
        self.pending[eng].append(o)
        return o

    def dma(self, eng, ch, fn, r=(), w=()):
        self._chsem(ch)
        o = _Op()
        o.eng = eng
        o.fn = fn
        o.eord = self.ecount[eng]
        self.ecount[eng] += 1
        o.isdma = True
        o.qkey = ch
        self.qcount[ch] += 16
        o.qval = self.qcount[ch]
        self._access(o, r, False)
        self._access(o, w, True)
        self.pending[eng].append(o)
        return o

    def raw(self, eng, fn):
        o = _Op()
        o.eng = eng
        o.fn = fn
        o.eord = self.ecount[eng]
        self.ecount[eng] += 1
        o.noinst = True
        self.pending[eng].append(o)
        return o

    def _waits(self, o):
        acc = {}
        for d in o.deps:
            if d.isdma:
                sem = self.dmasem[d.qkey]
                val = d.qval
            else:
                tgt = d if d.icount is not None else d.nextinc
                sem = self.engsem[d.eng]
                val = tgt.icount
            k = id(sem)
            if k not in acc or acc[k][1] < val:
                acc[k] = (sem, val)
        return list(acc.values())

    def flush(self):
        for e in "tvsg":
            ops = [o for o in self.pending[e] if not o.noinst and not o.isdma]
            if ops:
                ops[-1].inc = True
            for o in ops:
                if o.inc:
                    self.inccount[e] += 1
                    o.icount = self.inccount[e]
            nxt = None
            for o in reversed(ops):
                if o.icount is not None:
                    nxt = o
                o.nextinc = nxt
        for e in ("y", "g", "v", "s", "t"):
            ops = self.pending[e]
            if not ops:
                continue
            hw = self.hw[e]
            dmsems = self.dmasem
            emsems = self.engsem

            def body(eng, ops=ops, e=e, hw=hw):
                for o in ops:
                    if o.noinst:
                        o.fn(eng)
                        continue
                    for (sem, val) in self._waits(o):
                        k = id(sem)
                        if hw.get(k, -1) < val:
                            eng.wait_ge(sem, val)
                            hw[k] = val
                    inst = o.fn(eng)
                    if o.isdma:
                        inst.then_inc(dmsems[o.qkey], 16)
                    elif o.icount is not None:
                        inst.then_inc(emsems[e], 1)

            getattr(self.block, ENG_NAMES[e])(body)
            self.pending[e] = []

    def barrier(self):
        self.flush()
        counts = dict(self.inccount)
        qs = dict(self.qcount)

        for e in "tvsgy":
            def f(eng, e=e, counts=counts, qs=qs):
                for f2 in "tvsg":
                    if f2 != e and counts[f2] > 0:
                        eng.wait_ge(self.engsem[f2], counts[f2])
                for q, v in qs.items():
                    if v > 0:
                        eng.wait_ge(self.dmasem[q], v)
            self.raw(e, f)
            for f2 in "tvsg":
                if f2 != e:
                    k = id(self.engsem[f2])
                    self.hw[e][k] = max(self.hw[e].get(k, -1), counts[f2])
            for q, v in qs.items():
                k = id(self.dmasem[q])
                self.hw[e][k] = max(self.hw[e].get(k, -1), v)


_ISZ = {F32: 4, BF16: 2, I32: 4}


class Arena:

    def __init__(self, lo, hi):
        self.spans = [(lo, hi)]

    def alloc(self, n, align=32):
        n = (n + align - 1) // align * align
        for i, (lo, hi) in enumerate(self.spans):
            alo = (lo + align - 1) // align * align
            if hi - alo >= n:
                pre = (lo, alo) if alo > lo else None
                post = (alo + n, hi) if alo + n < hi else None
                repl = ([pre] if pre else []) + ([post] if post else [])
                self.spans[i:i + 1] = repl
                return (alo, alo + n)
        raise MemoryError(f"sbuf arena: cannot alloc {n} bytes; free={self.spans}")

    def free(self, rng):
        lo, hi = rng
        self.spans.append((lo, hi))
        self.spans.sort()
        merged = []
        for s in self.spans:
            if merged and merged[-1][1] >= s[0]:
                merged[-1] = (merged[-1][0], max(merged[-1][1], s[1]))
            else:
                merged.append(list(s) and s)
        self.spans = [tuple(s) for s in merged]


class Grp:
    def __init__(self, arena):
        self.arena = arena
        self.rngs = []

    def close(self):
        for r_ in self.rngs:
            self.arena.free(r_)
        self.rngs = []


def build_program():
    nc = bass.Bass("TRN2", target_bir_lowering=False)

    xown = nc.dram_tensor("xown", [TOK, D], F32, kind="ExternalInput").ap()
    xoth = nc.dram_tensor("xoth", [TOK, D], F32, kind="ExternalInput").ap()
    wq = nc.dram_tensor("wq", [D, D], F32, kind="ExternalInput").ap()
    bq = nc.dram_tensor("bq", [D], F32, kind="ExternalInput").ap()
    wl = nc.dram_tensor("wl", [D, LAT], F32, kind="ExternalInput").ap()
    bl = nc.dram_tensor("bl", [LAT], F32, kind="ExternalInput").ap()
    wk = nc.dram_tensor("wk", [LAT, D], F32, kind="ExternalInput").ap()
    bk = nc.dram_tensor("bk", [D], F32, kind="ExternalInput").ap()
    wv = nc.dram_tensor("wv", [LAT, D], F32, kind="ExternalInput").ap()
    bv = nc.dram_tensor("bv", [D], F32, kind="ExternalInput").ap()
    wo = nc.dram_tensor("wo", [D, D], F32, kind="ExternalInput").ap()
    bo = nc.dram_tensor("bo", [D], F32, kind="ExternalInput").ap()
    wg = nc.dram_tensor("wg", [HID, D], I32, kind="ExternalInput").ap()
    wu = nc.dram_tensor("wu", [HID, D], I32, kind="ExternalInput").ap()
    wd = nc.dram_tensor("wd", [D, HID], I32, kind="ExternalInput").ap()
    out = nc.dram_tensor("out", [TOK, D], F32, kind="ExternalOutput").ap()
    xp = nc.dram_tensor("xp", [TOK, D], F32, kind="Internal").ap()
    xpd = {"key": "xpd", "isz": 1, "psum": False}

    top = ExitStack()
    with top:
        block = top.enter_context(nc.Block())
        em = Em(nc, block, top)

        POOLW = 52800
        pool_h = top.enter_context(nc.sbuf_tensor("pool", [128, POOLW], I32))
        arena = Arena(0, POOLW * 4)
        perm = Grp(arena)

        def SB(name, shape, dt=F32, grp=None):
            ncols = int(np.prod(shape[1:]))
            nb = ncols * _ISZ[dt]
            rng = arena.alloc(nb)
            (grp or perm).rngs.append(rng)
            base = pool_h[:, rng[0] // 4: rng[1] // 4].bitcast(dt)
            if shape[0] != 128:
                base = base[0:shape[0], :]
            base = base[:, 0:ncols]
            return {"a": base, "key": name, "isz": _ISZ[dt], "psum": False, "cols": ncols}

        def PS(name, shape, dt=F32, stack=None):
            h = (stack or top).enter_context(nc.psum_tensor(name, shape, dt))
            return {"a": h, "key": name, "isz": _ISZ[dt], "psum": True,
                    "cols": int(np.prod(shape[1:]))}

        def _lp(f):
            def g(eng):
                with nc.allow_low_precision(reason="softmax 1/Z broadcast uses bf16"):
                    return f(eng)
            return g

        def full(b):
            return Em.r(b, 0, b["cols"])

        def view(buf, off, cols):
            return {"a": buf["a"][:, off:off + cols], "key": buf["key"],
                    "isz": buf["isz"], "psum": buf["psum"], "cols": cols,
                    "off": buf.get("off", 0) + off}

        rr = Em.r
        ap = lambda b: b["a"]

        ident = SB("ident", [128, 128], BF16)
        onescol = SB("onescol", [128, 1], BF16)
        onesw = SB("onesw", [65, 128], BF16)
        negm = SB("negm", [128, 1], F32)
        epsb = SB("epsb", [128, 1], F32)

        es_ac = Grp(arena)
        es_bd = Grp(arena)
        bk_sb = SB("bk_sb", [128, EC], F32, es_bd)
        bv16 = SB("bv16", [128, EC], BF16, es_bd)
        bo_row = SB("bo_row", [1, D], F32, es_bd)
        latT = [SB(f"latT{l}", [128, SEQ], BF16, es_bd) for l in range(LC)]
        qT = [SB(f"qT{e}", [128, TOK], BF16, es_bd) for e in range(EC)]
        wk_sb = SB("wk_sb", [128, LC * D], BF16, es_bd)
        wv_sb = SB("wv_sb", [128, LC * D], BF16, es_bd)
        iotaA = SB("iotaA", [128, 128], I32, es_ac)
        iotaB = SB("iotaB", [128, 128], I32, es_ac)
        bq_sb = SB("bq_sb", [128, EC], F32, es_ac)
        bl_sb = SB("bl_sb", [128, LC], F32, es_ac)

        em.op("g", lambda g: g.iota(ap(iotaA)[:], [[1, 128]], channel_multiplier=0),
              w=[full(iotaA)])
        em.op("g", lambda g: g.iota(ap(iotaB)[:], [[0, 128]], channel_multiplier=1),
              w=[full(iotaB)])
        em.op("v", lambda v: v.tensor_tensor(ap(ident)[:], ap(iotaA)[:], ap(iotaB)[:],
                                             ALU.is_equal),
              r=[full(iotaA), full(iotaB)], w=[full(ident)])
        em.op("g", lambda g: g.memset(ap(onescol)[:], 1.0), w=[full(onescol)])
        em.op("g", lambda g: g.memset(ap(onesw)[:], 1.0), w=[full(onesw)])
        em.op("g", lambda g: g.memset(ap(negm)[:], MSHIFT), w=[full(negm)])
        em.op("g", lambda g: g.memset(ap(epsb)[:], EPS), w=[full(epsb)])

        em.dma("g", "bq_sb", lambda g: g.dma_start(ap(bq_sb)[:], bq.rearrange("(c p) -> p c", p=128),
                                                allow_slow_non_contiguous=True), w=[full(bq_sb)])
        em.dma("g", "bk_sb", lambda g: g.dma_start(ap(bk_sb)[:], bk.rearrange("(c p) -> p c", p=128),
                                                allow_slow_non_contiguous=True), w=[full(bk_sb)])
        em.dma("g", "bl_sb", lambda g: g.dma_start(ap(bl_sb)[:], bl.rearrange("(c p) -> p c", p=128),
                                                allow_slow_non_contiguous=True), w=[full(bl_sb)])
        em.dma("g", "bv16", lambda g: g.dma_start(ap(bv16)[:], bv.rearrange("(c p) -> p c", p=128),
                                                allow_slow_non_contiguous=True), w=[full(bv16)])
        em.dma("g", "bo_row", lambda g: g.dma_start(ap(bo_row)[:], bo.rearrange("(a n) -> a n", a=1)),
               w=[full(bo_row)])

        def wload(dst, src, chunks):
            em.dma("g", dst["key"],
                   lambda g, d=dst, s=src, c=chunks: g.dma_start(
                       ap(d)[:].rearrange("p (c n) -> p c n", c=c),
                       s.rearrange("(c p) n -> p c n", p=128)),
                   w=[full(dst)])

        wq_sb = SB("wq_sb", [128, EC * D], BF16, es_ac)
        wl_sb = SB("wl_sb", [128, EC * LAT], BF16, es_ac)
        NXS = 6
        xs = [SB(f"xs{i}", [128, D], F32, es_ac) for i in range(NXS)]
        stt = [SB(f"stt{i}", [128, 12], F32, es_ac) for i in range(NXS)]
        mvb = [SB(f"mvb{i}", [128, 2], F32, es_ac) for i in range(NXS)]
        sdb = [SB(f"sdb{i}", [128, 1], F32, es_ac) for i in range(NXS)]
        rsb = [SB(f"rsb{i}", [128, 1], F32, es_ac) for i in range(NXS)]
        xnt = [SB(f"xnt{i}", [128, D], BF16, es_ac) for i in range(2)]
        xn1T_t = SB("xn1T", [128, EC * SEQ], BF16, es_ac)
        xn1T = [view(xn1T_t, e * SEQ, SEQ) for e in range(EC)]

        ps_ac = ExitStack()
        top.enter_context(ps_ac)
        ps_tr = PS("ps_tr", [128, 2048], BF16, ps_ac)
        ps_mm = PS("ps_mm", [128, 1024], F32, ps_ac)

        wload(wq_sb, wq, EC)
        wload(wl_sb, wl, EC)
        wload(wk_sb, wk, LC)
        wload(wv_sb, wv, LC)

        def ln_s1(tile, st, mv, sd):
            em.op("v", lambda v, a=tile, b=st: v.bn_stats(ap(b)[:, 0:6], ap(a)[:, 0:512]),
                  r=[rr(tile, 0, 512)], w=[rr(st, 0, 6)])
            em.op("v", lambda v, a=tile, b=st: v.bn_stats(ap(b)[:, 6:12], ap(a)[:, 512:1024]),
                  r=[rr(tile, 512, 1024)], w=[rr(st, 6, 12)])
            em.op("v", lambda v, b=st, m=mv: v.bn_aggr(ap(m)[:], ap(b)[:, 0:12]),
                  r=[rr(st, 0, 12)], w=[full(mv)])
            em.op("s", lambda s, m=mv, d=sd: s.activation(ap(d)[:], ap(m)[:, 1:2], AF.Sqrt,
                                                          bias=ap(epsb)[:]),
                  r=[rr(mv, 1, 2), full(epsb)], w=[full(sd)])

        def ln_rec(sd, rs):
            em.op("v", lambda v, d=sd, r_=rs: v.reciprocal(ap(r_)[:], ap(d)[:]),
                  r=[full(sd)], w=[full(rs)])

        def ln_stats(tile, st, mv, sd, rs):
            em.op("v", lambda v, a=tile, b=st: v.bn_stats(ap(b)[:, 0:6], ap(a)[:, 0:512]),
                  r=[rr(tile, 0, 512)], w=[rr(st, 0, 6)])
            em.op("v", lambda v, a=tile, b=st: v.bn_stats(ap(b)[:, 6:12], ap(a)[:, 512:1024]),
                  r=[rr(tile, 512, 1024)], w=[rr(st, 6, 12)])
            em.op("v", lambda v, b=st, m=mv: v.bn_aggr(ap(m)[:], ap(b)[:, 0:12]),
                  r=[rr(st, 0, 12)], w=[full(mv)])
            em.op("s", lambda s, m=mv, d=sd: s.activation(ap(d)[:], ap(m)[:, 1:2], AF.Sqrt,
                                                          bias=ap(epsb)[:]),
                  r=[rr(mv, 1, 2), full(epsb)], w=[full(sd)])
            em.op("v", lambda v, d=sd, r_=rs: v.reciprocal(ap(r_)[:], ap(d)[:]),
                  r=[full(sd)], w=[full(rs)])

        def transpose_tile(src_tile, dst_t, width, col, ping, ps_pool, nch=EC):
            half = 1024 * ping
            for e in range(nch):
                em.op("t", lambda t, a=src_tile, e=e, h=half, p=ps_pool: t.transpose(
                    ap(p)[:, h + e * 128: h + (e + 1) * 128],
                    ap(a)[:, e * 128:(e + 1) * 128], ap(ident)[:]),
                    r=[rr(src_tile, e * 128, (e + 1) * 128), full(ident)],
                    w=[rr(ps_pool, half + e * 128, half + (e + 1) * 128)])
            em.op("v", lambda v, h=half, d=dst_t, c=col, p=ps_pool, n=nch, wd=width:
                  v.tensor_copy(
                      ap(d).rearrange("p (e n) -> p e n", e=ap(d).shape[-1] // wd)[:, 0:n, c:c + 128],
                      ap(p)[:, h:h + n * 128].rearrange("p (e n) -> p e n", e=n)),
                  r=[rr(ps_pool, half, half + nch * 128)],
                  w=[full(dst_t)])

        def a_finish(j):
            buf = j % NXS
            ln_rec(sdb[buf], rsb[buf])
            em.op("v", lambda v, b=buf, o=j % 2: v.tensor_scalar(
                ap(xnt[o])[:], ap(xs[b])[:], ap(mvb[b])[:, 0:1], ap(rsb[b])[:],
                ALU.subtract, ALU.mult),
                r=[full(xs[buf]), rr(mvb[buf], 0, 1), full(rsb[buf])],
                w=[full(xnt[j % 2])])
            transpose_tile(xnt[j % 2], xn1T_t, SEQ, j * 128, j % 2, ps_tr)

        for st_i in range(NT_SEQ):
            buf = st_i % NXS
            src = xown if st_i < NT_OWN else xoth
            row = (st_i % NT_OWN) * 128
            em.dma("y", f"xs{buf}", lambda s, b=buf, sr=src, r_=row: s.dma_start(
                ap(xs[b])[:], sr[r_:r_ + 128, :]), w=[full(xs[buf])])
            ln_s1(xs[buf], stt[buf], mvb[buf], sdb[buf])
            if st_i >= 2:
                a_finish(st_i - 2)
        a_finish(NT_SEQ - 2)
        a_finish(NT_SEQ - 1)

        pp = [0]

        def proj_fm(dst_list, w_sb, wwidth, kchunks, src_list, src_cols, out_chunks,
                    bias, ncols, col_off=0):
            for oc in range(out_chunks):
                for nb in range(ncols // 512):
                    psl = (pp[0] % 2) * 512
                    pp[0] += 1
                    for kc in range(kchunks):
                        em.op("t", lambda t, oc=oc, nb=nb, kc=kc, psn=psl, w_=w_sb, ww=wwidth,
                              sl=src_list, co=col_off, K=kchunks: t.matmul(
                                  ap(ps_mm)[:, psn:psn + 512],
                                  ap(w_)[:, kc * ww + oc * 128: kc * ww + oc * 128 + 128],
                                  ap(sl[kc])[:, co + nb * 512: co + (nb + 1) * 512],
                                  start=(kc == 0), stop=(kc == K - 1)),
                              r=[rr(w_sb, kc * wwidth + oc * 128, kc * wwidth + oc * 128 + 128),
                                 rr(src_list[kc], col_off + nb * 512, col_off + (nb + 1) * 512)],
                              w=[rr(ps_mm, psl, psl + 512)])
                    em.op("v", lambda v, oc=oc, nb=nb, psn=psl, dl=dst_list, b=bias:
                          v.tensor_scalar(ap(dl[oc])[:, nb * 512:(nb + 1) * 512],
                                          ap(ps_mm)[:, psn:psn + 512],
                                          ap(b)[:, oc:oc + 1], None, ALU.add),
                          r=[rr(ps_mm, psl, psl + 512), rr(bias, oc, oc + 1)],
                          w=[rr(dst_list[oc], nb * 512, (nb + 1) * 512)])

        proj_fm(latT, wl_sb, LAT, EC, xn1T, SEQ, LC, bl_sb, SEQ)

        proj_fm(qT, wq_sb, D, EC, xn1T, SEQ, EC, bq_sb, TOK, col_off=0)

        em.barrier()
        em.flush()
        ps_ac.close()
        es_ac.close()

        es_de = Grp(arena)
        wo_sb = SB("wo_sb", [128, EC * D], BF16, es_de)
        attnout = [SB(f"attnout{e}", [128, TOK], BF16, es_de) for e in range(EC)]
        bo2row = SB("bo2row", [1, D], BF16, es_de)
        bo2bc = SB("bo2bc", [128, D], BF16, es_de)
        es_d = Grp(arena)
        kP2 = [[SB(f"kP{j}_{i}", [128, SEQ], BF16, es_d) for i in range(2)] for j in range(2)]
        vP2 = [SB(f"vP{j}", [128, NKT * 192], BF16, es_d) for j in range(2)]
        pbr = SB("pbr", [128, 6 * 1024], BF16, es_d)
        rzb = SB("rzb", [128, 1024], F32, es_d)

        ps_d = ExitStack()
        top.enter_context(ps_d)
        ps_S = PS("ps_S", [128, 2048], F32, ps_d)
        ps_PV = PS("ps_PV", [128, 1024], F32, ps_d)
        ps_B = PS("ps_B", [128, 1024], F32, ps_d)

        wload(wo_sb, wo, EC)


        for j in range(2):
            em.op("g", lambda g, j=j: g.memset(ap(kP2[j][0])[64:128, :], 0.0),
                  w=[full(kP2[j][0])])
            em.op("g", lambda g, j=j: g.memset(ap(kP2[j][1])[0:64, :], 0.0),
                  w=[full(kP2[j][1])])
            em.op("g", lambda g, j=j: g.memset(
                ap(vP2[j]).rearrange("p (k n) -> p k n", k=NKT)[:, :, 64:128], 1.0),
                w=[full(vP2[j])])

        pvs = SB("pvs", [128, 1024], F32, es_d)
        def kv_pre(hp, kP, vP):
            for nb in range(SEQ // 512):
                psl = (nb % 2) * 512
                ops = []
                for kc in range(LC):
                    yield lambda kc=kc, nb=nb, psl=psl, hp=hp: em.op(
                        "t", lambda t, nb=nb, kc=kc, psn=psl, hp=hp: t.matmul(
                            ap(ps_B)[:, psn:psn + 512],
                            ap(wk_sb)[:, kc * D + hp * 128: kc * D + hp * 128 + 128],
                            ap(latT[kc])[:, nb * 512:(nb + 1) * 512],
                            start=(kc == 0), stop=(kc == LC - 1)),
                        r=[rr(wk_sb, kc * D + hp * 128, kc * D + hp * 128 + 128),
                           rr(latT[kc], nb * 512, (nb + 1) * 512)],
                        w=[rr(ps_B, psl, psl + 512)])
                for i in range(2):
                    yield lambda i=i, nb=nb, psl=psl, hp=hp, kP=kP: em.op(
                        "v", lambda v, nb=nb, psn=psl, hp=hp, i=i, kP=kP: v.tensor_scalar(
                            ap(kP[i])[64 * i:64 * i + 64, nb * 512:(nb + 1) * 512],
                            ap(ps_B)[64 * i:64 * i + 64, psn:psn + 512],
                            ap(bk_sb)[64 * i:64 * i + 64, hp:hp + 1], None, ALU.add),
                        r=[rr(ps_B, psl, psl + 512), rr(bk_sb, hp, hp + 1)],
                        w=[rr(kP[i], nb * 512, (nb + 1) * 512)])
            for kt in range(NKT):
                psl = (kt % 2) * 512
                for kc in range(LC):
                    yield lambda kt=kt, kc=kc, psl=psl, hp=hp: em.op(
                        "t", lambda t, kt=kt, kc=kc, psn=psl, hp=hp: t.matmul(
                            ap(ps_B)[:, psn:psn + 128],
                            ap(latT[kc])[:, kt * 128:(kt + 1) * 128],
                            ap(wv_sb)[:, kc * D + hp * 128: kc * D + hp * 128 + 128],
                            start=(kc == 0), stop=(kc == LC - 1)),
                        r=[rr(latT[kc], kt * 128, (kt + 1) * 128),
                           rr(wv_sb, kc * D + hp * 128, kc * D + hp * 128 + 128)],
                        w=[rr(ps_B, psl, psl + 128)])
                for i in range(2):
                    yield lambda kt=kt, psl=psl, i=i, vP=vP: em.op(
                        "v", lambda v, kt=kt, psn=psl, i=i, vP=vP: v.tensor_copy(
                            ap(vP)[:, kt * 192 + 128 * i:kt * 192 + 128 * i + 64],
                            ap(ps_B)[:, psn + 64 * i:psn + 64 * i + 64]),
                        r=[rr(ps_B, psl, psl + 128)],
                        w=[rr(vP, kt * 192 + 128 * i, kt * 192 + 128 * i + 64)])

        for g_op in kv_pre(0, kP2[0], vP2[0]):
            g_op()

        steps = [(hp, side, qh, kt)
                 for hp in range(NH // 2) for side in range(2)
                 for qh in range(2) for kt in range(NKT)]
        NS = len(steps)
        pres = {hp: kv_pre(hp + 1, kP2[(hp + 1) % 2], vP2[(hp + 1) % 2])
                for hp in range(NH // 2 - 1)}

        def s_emit(s):
            hp, side, qh, kt = steps[s]
            kP = kP2[hp % 2]
            so = (s % 2) * 1024
            for qc in range(2):
                em.op("t", lambda t, kt=kt, qc=qc, side=side, so=so, qh=qh, hp=hp, kP=kP: t.matmul(
                    ap(ps_S)[:, so + qc * 512: so + (qc + 1) * 512],
                    ap(kP[side])[:, kt * 128:(kt + 1) * 128],
                    ap(qT[hp])[:, qh * 1024 + qc * 512: qh * 1024 + (qc + 1) * 512],
                    start=True, stop=True),
                    r=[rr(kP[side], kt * 128, (kt + 1) * 128),
                       rr(qT[hp], qh * 1024 + qc * 512, qh * 1024 + (qc + 1) * 512)],
                    w=[rr(ps_S, so + qc * 512, so + (qc + 1) * 512)])

        def exp_emit(s):
            so = (s % 2) * 1024
            slot = (s % 6) * 1024
            em.op("s", lambda sc, so=so, sl=slot: sc.activation(
                ap(pbr)[:, sl:sl + 1024], ap(ps_S)[:, so:so + 1024],
                AF.Exp, bias=ap(negm)[:], scale=0.125),
                r=[rr(ps_S, so, so + 1024), full(negm)],
                w=[rr(pbr, slot, slot + 1024)])

        s_emit(0)
        s_emit(1)
        exp_emit(0)
        deferred = []
        for s in range(NS):
            hp, side, qh, kt = steps[s]
            vP = vP2[hp % 2]
            if s + 2 < NS:
                s_emit(s + 2)
            if s + 1 < NS:
                exp_emit(s + 1)
            if kt % 2 == 0 and deferred:
                for th in deferred.pop(0):
                    th()
            pre = pres.get(hp)
            if pre is not None:
                for _ in range(2):
                    g_op = next(pre, None)
                    if g_op is not None:
                        g_op()
            slot = (s % 6) * 1024
            vco = 192 * kt + 64 * side
            for qc in range(2):
                em.op("t", lambda t, kt=kt, qc=qc, vP=vP, vco=vco, sl=slot: t.matmul(
                    ap(ps_PV)[:, qc * 512:(qc + 1) * 512],
                    ap(vP)[:, vco:vco + 128],
                    ap(pbr)[:, sl + qc * 512: sl + (qc + 1) * 512],
                    start=(kt == 0), stop=(kt == NKT - 1)),
                    r=[rr(vP, vco, vco + 128),
                       rr(pbr, slot + qc * 512, slot + (qc + 1) * 512)],
                    w=[rr(ps_PV, qc * 512, (qc + 1) * 512)])
            if kt == NKT - 1:
                em.op("v", lambda v: v.tensor_copy(ap(pvs)[0:128, :],
                                                   ap(ps_PV)[0:128, 0:1024]),
                      r=[rr(ps_PV, 0, 1024)], w=[full(pvs)])

                zr = 64 * (1 - side)
                vr = 64 * side

                def th_rz(c, zr=zr, vr=vr):
                    em.op("v", lambda v, c=c: v.reciprocal(
                        ap(rzb)[vr:vr + 64, c * 256:(c + 1) * 256],
                        ap(pvs)[zr:zr + 64, c * 256:(c + 1) * 256]),
                        r=[full(pvs)], w=[full(rzb)])

                def th_tt(hp=hp, qh=qh, vr=vr):
                    em.op("v", lambda v: v.tensor_tensor(
                        ap(attnout[hp])[vr:vr + 64, qh * 1024:(qh + 1) * 1024],
                        ap(pvs)[vr:vr + 64, 0:1024],
                        ap(rzb)[vr:vr + 64, 0:1024], ALU.mult),
                        r=[full(pvs), full(rzb)],
                        w=[full(attnout[hp])])

                deferred = [[lambda c=c: th_rz(c)] for c in range(4)] + [[th_tt]]

        while deferred:
            for th in deferred.pop(0):
                th()
        for half_i in range(2):
            for kc in range(EC):
                em.op("t", lambda t, kc=kc, hi=half_i: t.matmul(
                    ap(ps_B)[0:1, hi * 512:(hi + 1) * 512], ap(bv16)[:, kc:kc + 1],
                    ap(wo_sb)[:, kc * D + hi * 512: kc * D + (hi + 1) * 512],
                    start=(kc == 0), stop=(kc == EC - 1)),
                    r=[rr(bv16, kc, kc + 1), rr(wo_sb, kc * D + half_i * 512,
                                                kc * D + (half_i + 1) * 512)],
                    w=[rr(ps_B, half_i * 512, (half_i + 1) * 512)])
        em.op("v", lambda v: v.tensor_tensor(ap(bo2row)[:], ap(ps_B)[0:1, 0:1024],
                                             ap(bo_row)[:], ALU.add),
              r=[rr(ps_B, 0, 1024), full(bo_row)], w=[full(bo2row)])
        for half_i in range(2):
            em.op("t", lambda t, hi=half_i: t.matmul(
                ap(ps_B)[:, hi * 512:(hi + 1) * 512], ap(onesw)[0:1, 0:128],
                ap(bo2row)[0:1, hi * 512:(hi + 1) * 512], start=True, stop=True),
                r=[full(onesw), rr(bo2row, half_i * 512, (half_i + 1) * 512)],
                w=[rr(ps_B, half_i * 512, (half_i + 1) * 512)])
        em.op("v", lambda v: v.tensor_copy(ap(bo2bc)[:], ap(ps_B)[:, 0:1024]),
              r=[rr(ps_B, 0, 1024)], w=[full(bo2bc)])
        em.barrier()
        em.flush()
        ps_d.close()
        es_d.close()
        es_bd.close()

        es_wds = Grp(arena)
        wd_stage = [SB(f"wds{i}", [128, HID], BF16, es_wds) for i in range(2)]
        es_wdT = Grp(arena)
        wdT_t = SB("wdT", [128, HC * D], BF16, es_wdT)
        wdT = [view(wdT_t, h * D, D) for h in range(HC)]
        for et in range(2):
            em.dma("g", f"wds{et}", lambda g, et=et: g.dma_start(
                ap(wd_stage[et])[:], wd[et * 128:(et + 1) * 128, :]),
                w=[full(wd_stage[et])])
        es_xn2 = Grp(arena)
        xn2T_t = SB("xn2T", [128, EC * TOK], BF16, es_xn2)
        xn2T = [view(xn2T_t, e * TOK, TOK) for e in range(EC)]
        es_e = Grp(arena)
        y1s = [SB(f"y1s{i}", [128, D], F32, es_e) for i in range(3)]
        xt2 = [SB(f"xt2{i}", [128, D], F32, es_e) for i in range(3)]
        xps = [SB(f"xps{i}", [128, D], F32, es_e) for i in range(3)]
        xpn = [SB(f"xpn{i}", [128, D], F32, es_e) for i in range(3)]
        xn2t = [SB(f"xn2t{i}", [128, D], BF16, es_e) for i in range(3)]
        st2 = [SB(f"st2_{i}", [128, 12], F32, es_e) for i in range(3)]
        mv2 = [SB(f"mv2_{i}", [128, 2], F32, es_e) for i in range(3)]
        sd2 = [SB(f"sd2_{i}", [128, 1], F32, es_e) for i in range(3)]
        rs2 = [SB(f"rs2_{i}", [128, 1], F32, es_e) for i in range(3)]
        st3 = [SB(f"st3_{i}", [128, 12], F32, es_e) for i in range(3)]
        mv3 = [SB(f"mv3_{i}", [128, 2], F32, es_e) for i in range(3)]
        sd3 = [SB(f"sd3_{i}", [128, 1], F32, es_e) for i in range(3)]
        rs3 = [SB(f"rs3_{i}", [128, 1], F32, es_e) for i in range(3)]

        ps_e = ExitStack()
        top.enter_context(ps_e)
        ps_y1 = PS("ps_y1", [128, 3072], F32, ps_e)
        ps_tr2 = PS("ps_tr2", [128, 2048], BF16, ps_e)

        def wd_transpose(et, buf):
            for g in range(HC // 8):
                for j in range(8):
                    hc = g * 8 + j
                    em.op("t", lambda t, b=buf, hc=hc, j=j: t.transpose(
                        ap(ps_tr2)[:, 1024 + j * 128: 1024 + j * 128 + 128],
                        ap(wd_stage[b])[:, hc * 128:(hc + 1) * 128], ap(ident)[:]),
                        r=[rr(wd_stage[buf], hc * 128, (hc + 1) * 128), full(ident)],
                        w=[rr(ps_tr2, 1024 + j * 128, 1024 + j * 128 + 128)])
                em.op("v", lambda v, g=g, et=et: v.tensor_copy(
                    ap(wdT_t).rearrange("p (h n) -> p h n", h=HC)[:, g * 8:(g + 1) * 8,
                                                                 et * 128:(et + 1) * 128],
                    ap(ps_tr2)[:, 1024:2048].rearrange("p (j n) -> p j n", j=8)),
                    r=[rr(ps_tr2, 1024, 2048)],
                    w=[full(wdT_t)])

        for tt in range(NT_OWN):
            buf = tt % 3
            for half_i in range(2):
                for dc in range(EC):
                    em.op("t", lambda t, tt=tt, dc=dc, hi=half_i, b=buf: t.matmul(
                        ap(ps_y1)[:, b * 1024 + hi * 512: b * 1024 + (hi + 1) * 512],
                        ap(attnout[dc])[:, tt * 128:(tt + 1) * 128],
                        ap(wo_sb)[:, dc * D + hi * 512: dc * D + (hi + 1) * 512],
                        start=(dc == 0), stop=(dc == EC - 1)),
                        r=[rr(attnout[dc], tt * 128, (tt + 1) * 128),
                           rr(wo_sb, dc * D + half_i * 512, dc * D + (half_i + 1) * 512)],
                        w=[rr(ps_y1, buf * 1024 + half_i * 512, buf * 1024 + (half_i + 1) * 512)])
            if tt >= 1:
                pv_ = (tt - 1) % 3
                transpose_tile(xn2t[pv_], xn2T_t, TOK, (tt - 1) * 128, 0, ps_tr2)
            if tt % 2 == 0 and tt // 2 < EC:
                et = tt // 2
                wd_transpose(et, et % 2)
                if et + 2 < EC:
                    em.dma("g", f"wds{et % 2}", lambda g, et=et: g.dma_start(
                        ap(wd_stage[et % 2])[:], wd[(et + 2) * 128:(et + 3) * 128, :]),
                        w=[full(wd_stage[et % 2])])
            em.op("v", lambda v, b=buf: v.tensor_tensor(
                ap(y1s[b])[:], ap(ps_y1)[:, b * 1024:(b + 1) * 1024], ap(bo2bc)[:], ALU.add),
                r=[rr(ps_y1, buf * 1024, (buf + 1) * 1024), full(bo2bc)],
                w=[full(y1s[buf])])
            ln_stats(y1s[buf], st2[buf], mv2[buf], sd2[buf], rs2[buf])
            em.dma("y", f"xt2{buf}", lambda s, tt=tt, b=buf: s.dma_start(
                ap(xt2[b])[:], xown[tt * 128:(tt + 1) * 128, :]), w=[full(xt2[buf])])
            em.op("v", lambda v, b=buf: v.tensor_scalar(
                ap(xpn[b])[:], ap(y1s[b])[:], ap(mv2[b])[:, 0:1], ap(rs2[b])[:],
                ALU.subtract, ALU.mult),
                r=[full(y1s[buf]), rr(mv2[buf], 0, 1), full(rs2[buf])], w=[full(xpn[buf])])
            em.op("v", lambda v, b=buf: v.tensor_tensor(
                ap(xps[b])[:], ap(xpn[b])[:], ap(xt2[b])[:], ALU.add),
                r=[full(xpn[buf]), full(xt2[buf])], w=[full(xps[buf])])
            em.dma("y", f"xps{buf}", lambda s, tt=tt, b=buf: s.dma_start(
                xp[tt * 128:(tt + 1) * 128, :], ap(xps[b])[:]),
                r=[full(xps[buf])], w=[rr(xpd, tt, tt + 1)])
            ln_stats(xps[buf], st3[buf], mv3[buf], sd3[buf], rs3[buf])
            em.op("v", lambda v, b=buf: v.tensor_scalar(
                ap(xn2t[b])[:], ap(xps[b])[:], ap(mv3[b])[:, 0:1], ap(rs3[b])[:],
                ALU.subtract, ALU.mult),
                r=[full(xps[buf]), rr(mv3[buf], 0, 1), full(rs3[buf])], w=[full(xn2t[buf])])
        transpose_tile(xn2t[(NT_OWN - 1) % 3], xn2T_t, TOK, (NT_OWN - 1) * 128, 0, ps_tr2)

        em.barrier()
        em.flush()
        ps_e.close()
        es_e.close()
        es_de.close()
        es_wds.close()

        es_f = Grp(arena)
        hT = [SB(f"hT{h}", [128, TOK], BF16, es_f) for h in range(HC)]
        wg_st = [SB(f"wgst{i}", [128, D], BF16, es_f) for i in range(2)]
        wu_st = [SB(f"wust{i}", [128, D], BF16, es_f) for i in range(2)]
        wgT = [SB(f"wgT{e}", [128, 256], BF16, es_f) for e in range(EC)]
        wuT = [SB(f"wuT{e}", [128, 256], BF16, es_f) for e in range(EC)]
        sg = [SB(f"sg{i}", [128, 2048], BF16, es_f) for i in range(2)]

        ps_f = ExitStack()
        top.enter_context(ps_f)
        ps_g = PS("ps_g", [128, 2048], F32, ps_f)
        ps_u = PS("ps_u", [128, 1024], F32, ps_f)
        ps_ta = PS("ps_ta", [128, 1024], BF16, ps_f)
        ps_tb = PS("ps_tb", [128, 1024], BF16, ps_f)

        def stage_w(dst, src, ht):
            em.dma("g", dst["key"], lambda g, d=dst, s=src, ht=ht: g.dma_start(
                ap(d)[:], s[ht * 128:(ht + 1) * 128, :]), w=[full(dst)])

        def wtr(stage, win, ht, pst):
            slot = (ht % 2) * 128
            for e in range(EC):
                em.op("t", lambda t, a=stage, e=e, p=pst: t.transpose(
                    ap(p)[:, e * 128:(e + 1) * 128],
                    ap(a)[:, e * 128:(e + 1) * 128], ap(ident)[:]),
                    r=[rr(stage, e * 128, (e + 1) * 128), full(ident)],
                    w=[rr(pst, e * 128, (e + 1) * 128)])
            for e in range(EC):
                em.op("v", lambda v, e=e, d=win[e], sl=slot, p=pst: v.tensor_copy(
                    ap(d)[:, sl:sl + 128], ap(p)[:, e * 128:(e + 1) * 128]),
                    r=[rr(pst, e * 128, (e + 1) * 128)],
                    w=[rr(win[e], slot, slot + 128)])

        stage_w(wg_st[0], wg, 0)
        stage_w(wu_st[0], wu, 0)
        wtr(wg_st[0], wgT, 0, ps_ta)
        wtr(wu_st[0], wuT, 0, ps_tb)

        for ht in range(HC):
            slot = (ht % 2) * 128
            if ht + 1 < HC:
                stage_w(wg_st[(ht + 1) % 2], wg, ht + 1)
                stage_w(wu_st[(ht + 1) % 2], wu, ht + 1)
            for qc in range(4):
                for e in range(EC):
                    em.op("t", lambda t, qc=qc, e=e, sl=slot: t.matmul(
                        ap(ps_g)[:, qc * 512:(qc + 1) * 512],
                        ap(wgT[e])[:, sl:sl + 128],
                        ap(xn2T[e])[:, qc * 512:(qc + 1) * 512],
                        start=(e == 0), stop=(e == EC - 1)),
                        r=[rr(wgT[e], slot, slot + 128), rr(xn2T[e], qc * 512, (qc + 1) * 512)],
                        w=[rr(ps_g, qc * 512, (qc + 1) * 512)])
            if ht + 1 < HC:
                wtr(wg_st[(ht + 1) % 2], wgT, ht + 1, ps_ta)
            em.op("s", lambda s, b=ht % 2: s.activation(
                ap(sg[b])[:], ap(ps_g)[:], AF.Silu, scale=1.0 / 127.0),
                r=[rr(ps_g, 0, 2048)], w=[full(sg[ht % 2])])
            for qc in range(4):
                ups = (qc % 2) * 512
                for e in range(EC):
                    em.op("t", lambda t, qc=qc, e=e, sl=slot, u=ups: t.matmul(
                        ap(ps_u)[:, u:u + 512],
                        ap(wuT[e])[:, sl:sl + 128],
                        ap(xn2T[e])[:, qc * 512:(qc + 1) * 512],
                        start=(e == 0), stop=(e == EC - 1)),
                        r=[rr(wuT[e], slot, slot + 128), rr(xn2T[e], qc * 512, (qc + 1) * 512)],
                        w=[rr(ps_u, ups, ups + 512)])
                em.op("v", lambda v, ht=ht, qc=qc, u=ups, b=ht % 2: v.tensor_tensor(
                    ap(hT[ht])[:, qc * 512:(qc + 1) * 512], ap(ps_u)[:, u:u + 512],
                    ap(sg[b])[:, qc * 512:(qc + 1) * 512], ALU.mult),
                    r=[rr(ps_u, ups, ups + 512), rr(sg[ht % 2], qc * 512, (qc + 1) * 512)],
                    w=[rr(hT[ht], qc * 512, (qc + 1) * 512)])
            if ht + 1 < HC:
                wtr(wu_st[(ht + 1) % 2], wuT, ht + 1, ps_tb)

        em.barrier()
        em.flush()
        ps_f.close()
        es_xn2.close()

        xpr = [SB(f"xpr{i}", [128, D], F32, es_f) for i in range(3)]
        outb = [SB(f"outb{i}", [128, D], F32, es_f) for i in range(3)]
        t4b = [SB(f"t4b{i}", [128, D], F32, es_f) for i in range(2)]
        st4 = [SB(f"st4_{i}", [128, 12], F32, es_f) for i in range(2)]
        mv4 = [SB(f"mv4_{i}", [128, 2], F32, es_f) for i in range(2)]
        sd4 = [SB(f"sd4_{i}", [128, 1], F32, es_f) for i in range(2)]
        rs4 = [SB(f"rs4_{i}", [128, 1], F32, es_f) for i in range(2)]

        ps_f2 = ExitStack()
        top.enter_context(ps_f2)
        ps_y2 = PS("ps_y2", [128, 2048], F32, ps_f2)

        for tt in range(NT_OWN):
            buf = tt % 3
            pb_ = tt % 2
            em.dma("y", f"xpr{buf}", lambda s, tt=tt, b=buf: s.dma_start(
                ap(xpr[b])[:], xp[tt * 128:(tt + 1) * 128, :]),
                r=[rr(xpd, tt, tt + 1)], w=[full(xpr[buf])])
            for half_i in range(2):
                for hc in range(HC):
                    em.op("t", lambda t, tt=tt, hc=hc, hi=half_i, b=pb_: t.matmul(
                        ap(ps_y2)[:, b * 1024 + hi * 512: b * 1024 + (hi + 1) * 512],
                        ap(hT[hc])[:, tt * 128:(tt + 1) * 128],
                        ap(wdT[hc])[:, hi * 512:(hi + 1) * 512],
                        start=(hc == 0), stop=(hc == HC - 1)),
                        r=[rr(hT[hc], tt * 128, (tt + 1) * 128),
                           rr(wdT[hc], half_i * 512, (half_i + 1) * 512)],
                        w=[rr(ps_y2, pb_ * 1024 + half_i * 512, pb_ * 1024 + (half_i + 1) * 512)])
            em.op("v", lambda v, b=pb_, p=pb_: v.bn_stats(ap(st4[b])[:, 0:6],
                                                   ap(ps_y2)[:, p * 1024:p * 1024 + 512]),
                  r=[rr(ps_y2, pb_ * 1024, pb_ * 1024 + 512)], w=[rr(st4[pb_], 0, 6)])
            em.op("v", lambda v, b=pb_, p=pb_: v.bn_stats(ap(st4[b])[:, 6:12],
                                                   ap(ps_y2)[:, p * 1024 + 512:(p + 1) * 1024]),
                  r=[rr(ps_y2, pb_ * 1024 + 512, (pb_ + 1) * 1024)], w=[rr(st4[pb_], 6, 12)])
            em.op("v", lambda v, b=pb_: v.bn_aggr(ap(mv4[b])[:], ap(st4[b])[:, 0:12]),
                  r=[rr(st4[pb_], 0, 12)], w=[full(mv4[pb_])])
            em.op("s", lambda s, b=pb_: s.activation(ap(sd4[b])[:], ap(mv4[b])[:, 1:2], AF.Sqrt,
                                                     bias=ap(epsb)[:]),
                  r=[rr(mv4[pb_], 1, 2), full(epsb)], w=[full(sd4[pb_])])
            em.op("v", lambda v, b=pb_: v.reciprocal(ap(rs4[b])[:], ap(sd4[b])[:]),
                  r=[full(sd4[pb_])], w=[full(rs4[pb_])])
            em.op("v", lambda v, b=pb_, p=pb_: v.tensor_scalar(
                ap(t4b[b])[:], ap(ps_y2)[:, p * 1024:(p + 1) * 1024],
                ap(mv4[b])[:, 0:1], ap(rs4[b])[:], ALU.subtract, ALU.mult),
                r=[rr(ps_y2, pb_ * 1024, (pb_ + 1) * 1024), rr(mv4[pb_], 0, 1), full(rs4[pb_])],
                w=[full(t4b[pb_])])
            em.op("v", lambda v, b=buf, t4=pb_: v.tensor_tensor(
                ap(outb[b])[:], ap(t4b[t4])[:], ap(xpr[b])[:], ALU.add),
                r=[full(t4b[pb_]), full(xpr[buf])], w=[full(outb[buf])])
            em.dma("y", f"outb{buf}", lambda s, tt=tt, b=buf: s.dma_start(
                out[tt * 128:(tt + 1) * 128, :], ap(outb[b])[:]), r=[full(outb[buf])])

        finals = {ch: em.qcount[ch] for ch in em.qcount}
        def _final(eng, finals=finals):
            for ch, v in finals.items():
                eng.wait_ge(em.dmasem[ch], v)
        em.raw("y", _final)
        em.flush()
        ps_f2.close()

    nc.finalize()
    return nc


_NC_CACHE = None


def kernel(L0_attn_x, L0_attn_Wq, L0_attn_bq, L0_attn_Wl, L0_attn_bl,
           L0_attn_Wk, L0_attn_bk, L0_attn_Wv, L0_attn_bv, L0_attn_Wo,
           L0_attn_bo, L0_ffn_w_gate, L0_ffn_w_up, L0_ffn_w_down):
    global _NC_CACHE
    x = np.ascontiguousarray(np.asarray(L0_attn_x, dtype=np.float32))
    c = np.ascontiguousarray
    weights = dict(
        wq=c(np.asarray(L0_attn_Wq, np.float32)), bq=c(np.asarray(L0_attn_bq, np.float32)),
        wl=c(np.asarray(L0_attn_Wl, np.float32)), bl=c(np.asarray(L0_attn_bl, np.float32)),
        wk=c(np.asarray(L0_attn_Wk, np.float32)), bk=c(np.asarray(L0_attn_bk, np.float32)),
        wv=c(np.asarray(L0_attn_Wv, np.float32)), bv=c(np.asarray(L0_attn_bv, np.float32)),
        wo=c(np.asarray(L0_attn_Wo, np.float32)), bo=c(np.asarray(L0_attn_bo, np.float32)),
        wg=c(np.asarray(L0_ffn_w_gate, np.int32)), wu=c(np.asarray(L0_ffn_w_up, np.int32)),
        wd=c(np.asarray(L0_ffn_w_down, np.int32)),
    )
    in_maps = []
    for core in range(8):
        b, half = core // 2, core % 2
        m = dict(weights)
        m["xown"] = c(x[b, half * TOK:(half + 1) * TOK, :])
        m["xoth"] = c(x[b, (1 - half) * TOK:(2 - half) * TOK, :])
        in_maps.append(m)
    if _NC_CACHE is None:
        _NC_CACHE = build_program()
    res = run_bass_kernel_spmd(_NC_CACHE, in_maps, core_ids=list(range(8)))
    outp = np.empty((4, SEQ, D), np.float32)
    for core in range(8):
        b, half = core // 2, core % 2
        outp[b, half * TOK:(half + 1) * TOK, :] = res.results[core]["out"]
    return outp
